# version 10
# baseline (speedup 1.0000x reference)
import numpy as np
import ml_dtypes
import concourse.bass as bass
import concourse.mybir as mybir
from concourse.bass_utils import run_bass_kernel_spmd

F32 = mybir.dt.float32
BF16 = mybir.dt.bfloat16
I32 = mybir.dt.int32
AF = mybir.ActivationFunctionType
ALU = mybir.AluOpType
AX = mybir.AxisListType

B, N, D = 2, 4096, 2048
T = 1024
I = 4096
NT = D // 128
NCH = N // 512
KT = N // 128
ITL = I // 128
NHD = 4
S_ATT = float(128.0 ** -0.5)
EPS = 1e-5
BF = ml_dtypes.bfloat16
GS = [3] * 10 + [2]


class Ctr:
    def __init__(self, nc, name):
        self.h = nc.alloc_semaphore(name=name)
        self.n = 0
        self._seen = []

    def bump(self, instr, k=1):
        assert id(instr) not in {id(o) for o in self._seen}, "double then_inc"
        self._seen.append(instr)
        instr.then_inc(self.h, k)
        self.n += k
        return self.n


class DStream:

    def __init__(self, nc, eng, name, nbuf=2):
        self.eng = eng
        self.sems = [nc.alloc_semaphore(name=f"{name}{i}") for i in range(nbuf)]
        self.cnt = [0] * nbuf
        self.i = 0

    def dma(self, out, in_):
        j = self.i % len(self.sems)
        instr = self.eng.dma_start(out=out, in_=in_)
        instr.then_inc(self.sems[j], 16)
        self.cnt[j] += 16
        self.i += 1
        return (j, self.cnt[j])

    def wait(self, weng, tok):
        j, v = tok
        weng.wait_ge(self.sems[j], v)

    def drain(self, weng):
        for j in range(len(self.sems)):
            if self.cnt[j]:
                weng.wait_ge(self.sems[j], self.cnt[j])


class Eng:

    _NOTRACK = {"wait_ge", "sem_clear", "last"}

    def __init__(self, eng):
        object.__setattr__(self, "_e", eng)
        object.__setattr__(self, "last", None)

    def __getattr__(self, n):
        f = getattr(self._e, n)
        if not callable(f) or n in self._NOTRACK or n.startswith("_"):
            return f
        def wrap(*a, **k):
            r = f(*a, **k)
            if hasattr(r, "then_inc"):
                object.__setattr__(self, "last", r)
            return r
        return wrap


def build_nc(phases=3, debug=0, stop=None):
    nc = bass.Bass("TRN2", target_bir_lowering=False, detect_race_conditions=False)

    xT = nc.dram_tensor("xT", [D, N], F32, kind="ExternalInput").ap()
    xslT = nc.dram_tensor("xslT", [D, T], F32, kind="ExternalInput").ap()
    x_sl = nc.dram_tensor("x_sl", [T, D], F32, kind="ExternalInput").ap()
    wqd = nc.dram_tensor("wqd", [D, 512], F32, kind="ExternalInput").ap()
    wod = nc.dram_tensor("wod", [D, D], F32, kind="ExternalInput").ap()
    wupT = nc.dram_tensor("wupT", [D, I], F32, kind="ExternalInput").ap()
    wdnT = nc.dram_tensor("wdnT", [I, D], F32, kind="ExternalInput").ap()
    bod = nc.dram_tensor("bod", [1, D], F32, kind="ExternalInput").ap()
    wshu = nc.dram_tensor("wshu", [256, I], F32, kind="ExternalInput").ap()
    wshd = nc.dram_tensor("wshd", [512, D], F32, kind="ExternalInput").ap()
    idbd = nc.dram_tensor("idbd", [128, 128], BF16, kind="ExternalInput").ap()
    idfd = nc.dram_tensor("idfd", [128, 128], F32, kind="ExternalInput").ap()
    outd = nc.dram_tensor("outd", [T, D], F32, kind="ExternalOutput").ap()

    tup_h = nc.dram_tensor("tup_h", [D, I], BF16, kind="Internal").ap()
    tdn_h = nc.dram_tensor("tdn_h", [I, D], BF16, kind="Internal").ap()
    cc1i = nc.dram_tensor("cc1i", [1, 2], F32, kind="Internal").ap()
    cc1o = nc.dram_tensor("cc1o", [8, 2], F32, kind="Internal").ap()
    cc2i = nc.dram_tensor("cc2i", [NHD * 128 * N], BF16, kind="Internal").ap()
    cc2o = nc.dram_tensor("cc2o", [4 * NHD * 128 * N], BF16, kind="Internal").ap()
    swzd = nc.dram_tensor("swzd", [4096], F32, kind="Internal").ap()

    dbg = {}
    if debug >= 1:
        dbg["qT"] = nc.dram_tensor("dbg_qT", [128, NHD * N], BF16, kind="ExternalOutput").ap()
        if phases >= 2:
            dbg["yn"] = nc.dram_tensor("dbg_yn", [NHD * 128, N], BF16, kind="ExternalOutput").ap()
            dbg["sig"] = nc.dram_tensor("dbg_sig", [NHD * 128, KT], F32, kind="ExternalOutput").ap()
            dbg["sml"] = nc.dram_tensor("dbg_sml", [128, 32 + 160], F32, kind="ExternalOutput").ap()
            dbg["bcf"] = nc.dram_tensor("dbg_bcf", [128, N], F32, kind="ExternalOutput").ap()
            dbg["rowq"] = nc.dram_tensor("dbg_rowq", [1, N], F32, kind="ExternalOutput").ap()
            dbg["rcol"] = nc.dram_tensor("dbg_rcol", [128, 32], F32, kind="ExternalOutput").ap()
            dbg["fst"] = nc.dram_tensor("dbg_fst", [32, 128], F32, kind="ExternalOutput").ap()
            dbg["swz"] = nc.dram_tensor("dbg_swz", [1, 4096], F32, kind="ExternalOutput").ap()
        if phases >= 3:
            dbg["xq"] = nc.dram_tensor("dbg_xq", [128, NT * T], BF16, kind="ExternalOutput").ap()
            dbg["hq"] = nc.dram_tensor("dbg_hq", [128, ITL * T], BF16, kind="ExternalOutput").ap()
            dbg["tm"] = nc.dram_tensor("dbg_tm", [128, 64], F32, kind="ExternalOutput").ap()

    SBASE = 16512

    def at(name, shape, dtype, off):
        return nc.alloc_sbuf_tensor_at(name, shape, dtype, offset=SBASE + off)

    qT = at("qT", [128, NHD * N], BF16, 0x00000)
    upsl = at("upsl", [128, 2 * NT * 512], BF16, 0x00000)
    dnsl = at("dnsl", [128, ITL * 512], BF16, 0x00000)
    xb = at("xb", [128, 2 * NT * 512], BF16, 0x08000)
    xq = at("xq", [128, NT * T], BF16, 0x08000)
    ffo = at("ffo", [128, 8 * D], BF16, 0x08000)
    wqb = at("wqb", [128, NT * 512], BF16, 0x10000)
    xsf = at("xsf", [128, 2 * T], F32, 0x10000)
    xsi = at("xsi", [128, T], I32, 0x12000)
    wob = at("wob", [128, NT * 512], BF16, 0x10000)
    rowq = at("rowq", [1, N], F32, 0x10000)
    wtf = at("wtf", [128, 2 * 4096], F32, 0x14000)
    ti32 = at("ti32", [128, 4096], I32, 0x1C000)
    tbo = at("tbo", [128, 4096], BF16, 0x20000)
    tbo2 = at("tbo2", [128, 4096], BF16, 0x22000)
    x2w = at("x2w", [128, 8192], BF16, 0x20000)
    v_sb = at("v_sb", [128, ITL * T], BF16, 0x14000)
    bcf = at("bcf", [128, N], F32, 0x24000)
    yraw = at("yraw", [128, N], F32, 0x28000)
    yallT = at("yallT", [128, 16 * T], BF16, 0x24000)
    xss = at("xss", [128, 2 * 512], F32, 0x2C000)
    vt = at("vt", [128, N], BF16, 0x2C000)
    xst = at("xst", [128, 2 * D], F32, 0x14000)
    sqi = at("sqi", [128, D], I32, 0x1C000)
    sqb = at("sqb", [128, 2 * D], BF16, 0x20000)
    s2bb = at("s2bb", [128, 2 * 512], F32, 0x2E000)
    pt = at("pt", [128, 2 * 1536], BF16, 0x2E000)
    ynb = at("ynb", [128, N], BF16, 0x2F800)
    bob = at("bob", [128, D], F32, 0x2F800)
    x2s = at("x2s", [128, 512], F32, 0x1C000)
    x2t = at("x2t", [128, 512], F32, 0x1C800)
    rbc = at("rbc", [128, 512], F32, 0x2E000)
    mbc = at("mbc", [128, 512], F32, 0x2E800)
    rowc = at("rowc", [1, 512], F32, 0x2F000)
    wbr = at("wbr", [1, 512], F32, 0x2F800)
    osb = at("osb", [128, 512], F32, 0x31800)
    fsc = at("fsc", [128, 512], F32, 0x32000)
    sgs = at("sgs", [128, 512], BF16, 0x32800)
    scol = at("scol", [128, 32], F32, 0x32C00)
    rcol = at("rcol", [128, 32], F32, 0x32C80)
    asmb = at("asmb", [128, 160], F32, 0x32D00)
    sm2 = at("sm2", [128, 64], F32, 0x32F80)
    tmv = at("tmv", [128, 64], F32, 0x33080)
    small = at("small", [128, 32], F32, 0x33180)
    identb = at("identb", [128, 128], BF16, 0x33200)
    identf = at("identf", [128, 128], F32, 0x33300)
    onesb = at("onesb", [128, 1], BF16, 0x33500)
    onesr = at("onesr", [1, 128], F32, 0x33520)
    sacc = at("sacc", [128, 256], F32, 0x33720)
    assert SBASE + 0x33720 + 1024 < 229344

    ps = nc.alloc_psum_tensor("ps", [128, 3584], F32)
    psb7 = nc.alloc_psum_tensor("psb7", [128, 1024], BF16)

    def bank(i, cols=512):
        return ps[:, i * 512:i * 512 + cols]

    c_pe = Ctr(nc, "c_pe")
    c_act = Ctr(nc, "c_act")
    c_dve = Ctr(nc, "c_dve")
    c_gp = Ctr(nc, "c_gp")

    TE = Eng(nc.tensor)
    SC = Eng(nc.scalar)
    VE = Eng(nc.vector)
    GP = Eng(nc.gpsimd)
    SY = Eng(nc.sync)

    dsC = DStream(nc, nc.sync, "sC", 1)
    dsSH = DStream(nc, nc.sync, "sSH", 2)
    dsWQ = DStream(nc, nc.gpsimd, "sWQ", 1)
    dsXB = DStream(nc, nc.gpsimd, "sXB", 2)
    dsCC = DStream(nc, nc.gpsimd, "sCC", 1)
    dsSW = DStream(nc, nc.sync, "sSW", 1)
    dsPUB = DStream(nc, nc.gpsimd, "sPUB", 1)
    dsTU = DStream(nc, nc.gpsimd, "sTU", 2)
    dsTD = DStream(nc, nc.gpsimd, "sTD", 2)
    dsTO = DStream(nc, nc.gpsimd, "sTO", 2)
    dsXS = DStream(nc, nc.sync, "sXS", 2)
    dsXF = DStream(nc, nc.sync, "sXF", 2)
    dsUP = DStream(nc, nc.sync, "sUP", 2)
    dsDN = DStream(nc, nc.sync, "sDN", 1)
    dsYA = DStream(nc, nc.sync, "sYA", 1)
    dsWO = DStream(nc, nc.gpsimd, "sWO", 2)
    dsXT = DStream(nc, nc.sync, "sXT", 2)
    dsST = DStream(nc, nc.scalar, "sST", 2)
    dsDBG = DStream(nc, nc.scalar, "sDBG", 1)
    ALL_DS = [dsC, dsSH, dsWQ, dsXB, dsCC, dsSW, dsPUB, dsTU, dsTD, dsTO,
              dsXS, dsXF, dsUP, dsDN, dsYA, dsWO, dsXT, dsST, dsDBG]

    ONESF = small[:, 0:1]
    EPSC = small[:, 1:2]
    ZEROC = small[:, 2:3]
    ACCFL = small[:, 3:4]
    BIASC = [small[:, 4 + i:5 + i] for i in range(NHD)]
    SUP = small[:, 8:9]
    SDN = small[:, 9:10]
    INVSUP = small[:, 10:11]
    INVSDN = small[:, 11:12]
    WCOL = [sm2[:, 16 + t:17 + t] for t in range(4)]
    TM_BETA = lambda q: tmv[:, 56 + q:57 + q]

    def vsp(n=4):
        for _ in range(n):
            VE.tensor_copy(sm2[:, 63:64], sm2[:, 62:63])

    def drain_all():
        for ds in ALL_DS:
            ds.drain(GP)
        GP.wait_ge(c_dve.h, c_dve.n)
        GP.wait_ge(c_pe.h, c_pe.n)
        GP.wait_ge(c_act.h, c_act.n)

    def fin(tag):
        if stop != tag:
            return False
        drain_all()
        return True

    with nc.Block() as block:
        dsC.dma(identb[:, :], idbd[:, :])
        t_id = dsC.dma(identf[:, :], idfd[:, :])
        t_shu = [dsSH.dma(wtf[:, 4096 * i:4096 * (i + 1)],
                          wshu[128 * i:128 * (i + 1), :]) for i in range(2)]
        VE.memset(ONESF, 1.0)
        VE.memset(EPSC, EPS)
        VE.memset(ZEROC, 0.0)
        VE.memset(onesb[:, :], 1.0)
        VE.memset(onesr[0:1, :], 1.0)
        f_const = c_dve.bump(VE.memset(scol[:, :], 0.0))
        SC.wait_ge(c_dve.h, f_const)
        SC.activation(out=sgs[:, 0:8], in_=small[:, 0:8], func=AF.Exp, scale=0.0,
                      bias=ZEROC, accum_out=ACCFL)
        dsSH.wait(VE, t_shu[0])
        VE.tensor_reduce(out=asmb[:, 150:151], in_=wtf[:, 0:4096], axis=AX.X,
                         op=ALU.add, apply_absolute_value=True)
        dsSH.wait(VE, t_shu[1])
        VE.tensor_reduce(out=asmb[:, 151:152], in_=wtf[:, 4096:8192], axis=AX.X,
                         op=ALU.add, apply_absolute_value=True)
        f_shu = c_dve.bump(VE.last)
        t_shd = [None] * 4
        sh_red = [0] * 4
        for i in range(4):
            if i == 0:
                SY.wait_ge(c_dve.h, f_shu)
            elif i >= 2:
                SY.wait_ge(c_dve.h, sh_red[i - 2])
            t_shd[i] = dsSH.dma(wtf[:, 2048 * i:2048 * (i + 1)],
                                wshd[128 * i:128 * (i + 1), :])
            dsSH.wait(VE, t_shd[i])
            VE.tensor_reduce(out=asmb[:, 152 + i:153 + i], in_=wtf[:, 2048 * i:2048 * (i + 1)],
                             axis=AX.X, op=ALU.add, apply_absolute_value=True)
            sh_red[i] = c_dve.bump(VE.last)
        VE.tensor_reduce(out=sm2[:, 0:1], in_=asmb[:, 150:152], axis=AX.X, op=ALU.add)
        VE.tensor_reduce(out=sm2[:, 1:2], in_=asmb[:, 152:156], axis=AX.X, op=ALU.add)
        vsp()
        f_shd = c_dve.bump(VE.last)
        if fin("ph0a"):
            return nc, dbg
        TE.wait_ge(c_dve.h, f_shd)
        dsC.wait(TE, t_id)
        TE.transpose(bank(6, 128)[0:2, :], sm2[:, 0:2], identf[:, :])
        p_t0 = c_pe.bump(TE.last)
        VE.wait_ge(c_pe.h, p_t0)
        VE.tensor_copy(asmb[0:2, 0:128], bank(6, 128)[0:2, :])
        VE.tensor_reduce(out=asmb[0:2, 128:129], in_=asmb[0:2, 0:128], axis=AX.X, op=ALU.add)
        vsp()
        f_par = c_dve.bump(VE.last)
        if fin("ph0b"):
            return nc, dbg

        t_wq = dsWQ.dma(wqb[:, :].rearrange("p (k c) -> p k c", k=NT),
                        wqd.rearrange("(k p) c -> p k c", p=128))
        dsWQ.wait(TE, t_wq)
        TE.wait_ge(c_dve.h, f_par)
        for t in range(4):
            for k in range(NT):
                TE.matmul(bank(2, 1), wqb[:, 512 * k + 128 * t:512 * k + 128 * (t + 1)],
                          onesb[:, :], start=(k == 0), stop=(k == NT - 1))
            pw = c_pe.bump(TE.last)
            VE.wait_ge(c_pe.h, pw)
            VE.tensor_copy(WCOL[t], bank(2, 1)[:, 0:1])
            vsp(2)
            wc_ve = c_dve.bump(VE.last)
            TE.wait_ge(c_dve.h, wc_ve)
        f_wcol = c_dve.n
        if fin("wbar"):
            return nc, dbg

        mx = [0] * NCH
        st_ev = [0] * NCH
        rows_rdy = [0] * NCH
        bc_mm = [0] * NCH
        qld = [0] * NCH
        x2mm = [0] * NCH
        q_ev = [0] * (4 * NCH)
        f_mr_list = [0] * NCH
        for c in range(NCH):
            half = (c % 2) * NT * 512

            def xbt(k, half=half):
                return xb[:, half + 512 * k:half + 512 * (k + 1)]

            if c >= 2:
                GP.wait_ge(c_pe.h, qld[c - 2])
            mx[c] = dsXB.dma(
                xb[:, half:half + NT * 512].rearrange("p (k t) -> p k t", k=NT),
                xT[:, 512 * c:512 * (c + 1)].rearrange("(k p) t -> p k t", p=128))

            dsXB.wait(TE, mx[c])
            if c >= 1:
                TE.wait_ge(c_act.h, st_ev[c - 1])
            for k in range(NT):
                TE.matmul(bank(0)[0:1, :], onesb[:, :], xbt(k), start=(k == 0), stop=(k == NT - 1))
            c_pe.bump(TE.last)
            dsXB.wait(VE, mx[c])
            if c >= 1:
                VE.wait_ge(c_pe.h, bc_mm[c - 1])
            if c >= 1:
                VE.wait_ge(c_pe.h, x2mm[c - 1])
            for j in range(4):
                VE.tensor_tensor(out=x2w[:, 2048 * j:2048 * (j + 1)],
                                 in0=xb[:, half + 2048 * j:half + 2048 * (j + 1)],
                                 in1=xb[:, half + 2048 * j:half + 2048 * (j + 1)],
                                 op=ALU.mult)
            f_x2 = c_dve.bump(VE.last)
            TE.wait_ge(c_dve.h, f_x2)
            for k in range(NT):
                TE.matmul(bank(1)[0:1, :], onesb[:, :], x2w[:, 512 * k:512 * (k + 1)],
                          start=(k == 0), stop=(k == NT - 1))
            x2mm[c] = c_pe.bump(TE.last)
            p_sxx = c_pe.n
            SC.wait_ge(c_pe.h, p_sxx)
            if c >= 1:
                SC.wait_ge(c_dve.h, q_ev[4 * (c - 1) + 3])
            SC.copy(out=rowc[0:1, :], in_=bank(0)[0:1, :])
            SC.copy(out=x2s[0:1, 0:512], in_=bank(1)[0:1, :])
            st_ev[c] = c_act.bump(SC.last)
            VE.wait_ge(c_act.h, st_ev[c])
            VE.tensor_scalar(out=rowc[0:1, :], in0=rowc[0:1, :], scalar1=1.0 / D,
                             scalar2=None, op0=ALU.mult)
            VE.tensor_scalar(out=x2s[0:1, 0:512], in0=x2s[0:1, 0:512], scalar1=1.0 / D,
                             scalar2=None, op0=ALU.mult)
            VE.tensor_tensor(out=x2t[0:1, 0:512], in0=rowc[0:1, :], in1=rowc[0:1, :], op=ALU.mult)
            VE.tensor_tensor(out=x2s[0:1, 0:512], in0=x2s[0:1, 0:512], in1=x2t[0:1, 0:512],
                             op=ALU.subtract)
            f_var = c_dve.bump(VE.last)
            SC.wait_ge(c_dve.h, f_var)
            SC.activation(out=x2s[0:1, 0:512], in_=x2s[0:1, 0:512], func=AF.Ln,
                          bias=EPSC[0:1, :], scale=1.0)
            SC.activation(out=x2s[0:1, 0:512], in_=x2s[0:1, 0:512], func=AF.Exp,
                          bias=ZEROC[0:1, :], scale=-0.5)
            a_r = c_act.bump(SC.last)
            VE.wait_ge(c_act.h, a_r)
            VE.tensor_tensor(out=x2t[0:1, 0:512], in0=rowc[0:1, :], in1=x2s[0:1, 0:512],
                             op=ALU.mult)
            VE.tensor_scalar(out=x2t[0:1, 0:512], in0=x2t[0:1, 0:512], scalar1=-1.0,
                             scalar2=None, op0=ALU.mult)
            f_mr_list[c] = c_dve.bump(VE.last)
            TE.wait_ge(c_dve.h, f_mr_list[c])
            TE.matmul(bank(5), onesr[0:1, :], x2s[0:1, 0:512], start=True, stop=True)
            TE.matmul(bank(6), onesr[0:1, :], x2t[0:1, 0:512], start=True, stop=True)
            bc_mm[c] = c_pe.bump(TE.last)
            SC.wait_ge(c_pe.h, bc_mm[c])
            SC.copy(out=rbc[:, :], in_=bank(5))
            SC.copy(out=mbc[:, :], in_=bank(6))
            rows_rdy[c] = c_act.bump(SC.last)

            for t in range(4):
                ii = 4 * c + t
                bq = bank(3 + (ii % 2))
                if ii >= 2:
                    TE.wait_ge(c_dve.h, q_ev[ii - 2])
                for k in range(NT):
                    TE.matmul(bq, wqb[:, 512 * k + 128 * t:512 * k + 128 * (t + 1)],
                              xbt(k), start=(k == 0), stop=(k == NT - 1))
                qp = c_pe.bump(TE.last)
                if t == 3:
                    qld[c] = qp
                VE.wait_ge(c_pe.h, qp)
                if t == 0:
                    VE.wait_ge(c_act.h, rows_rdy[c])
                VE.tensor_tensor(out=fsc[:, :], in0=bq, in1=rbc[:, :], op=ALU.mult)
                VE.scalar_tensor_tensor(out=qT[:, N * t + 512 * c:N * t + 512 * (c + 1)],
                                        in0=mbc[:, :], scalar=WCOL[t], in1=fsc[:, :],
                                        op0=ALU.mult, op1=ALU.add)
                q_ev[ii] = c_dve.bump(VE.last)
            if stop == "c" + str(c):
                break

        if stop is not None and stop.startswith("c") and fin(stop):
            return nc, dbg
        p1_dve = c_dve.n

        GP.wait_ge(c_dve.h, f_par)
        gm1 = dsCC.dma(cc1i[0:1, :].rearrange("a (b c) -> (a b) c", b=2),
                       asmb[0:2, 128:129])
        dsCC.wait(GP, gm1)
        cc1d = c_gp.bump(GP.collective_compute(
            "AllGather", ALU.bypass, replica_groups=[[0, 1, 2, 3, 4, 5, 6, 7]],
            ins=[cc1i[:, :]], outs=[cc1o[:, :]]))
        GP.wait_ge(c_gp.h, cc1d)
        gm2 = dsCC.dma(asmb[0:1, 132:148].rearrange("p (x y) -> p x y", x=8),
                       cc1o[:, :].rearrange("(u a) b -> u a b", u=1))

        if dbg.get("qT") is not None:
            SC.wait_ge(c_dve.h, p1_dve)
            dsDBG.dma(dbg["qT"][:, :], qT[:, :])

        if phases < 2:
            drain_all()
            return nc, dbg

        tern_up = [0] * NT
        tern_dn = [0] * ITL
        m_wup = [0] * NT
        m_wdn = [0] * ITL
        pub = [0] * NHD
        ms1 = [0] * NHD
        prb_last = [0] * NHD
        sig_red = [0] * NHD
        vtr = 0
        y_evac = 0
        b6_cons = 0
        f_yn_prev = 0
        y_last = {0: 0, 1: 0}
        f_scales = 0
        pending = {}
        f_ch_arr = [0] * NHD
        to_tok = [None] * (NT + ITL)
        tern_jobs = [("up", k) for k in range(NT)] + [("dn", k) for k in range(ITL)]
        t_cur = [0]

        def emit_scales():
            nonlocal f_scales, b6_cons
            dsCC.wait(VE, gm2)
            VE.tensor_reduce(out=asmb[0:1, 148:149], in_=asmb[0:1, 132:148:2],
                             axis=AX.X, op=ALU.add)
            VE.tensor_reduce(out=asmb[0:1, 149:150], in_=asmb[0:1, 133:148:2],
                             axis=AX.X, op=ALU.add)
            vsp()
            VE.tensor_scalar(out=asmb[0:1, 148:150], in0=asmb[0:1, 148:150],
                             scalar1=1.0 / (I * D), scalar2=EPS, op0=ALU.mult, op1=ALU.max)
            vsp()
            f_s = c_dve.bump(VE.last)
            TE.wait_ge(c_dve.h, f_s)
            TE.wait_ge(c_dve.h, b6_cons)
            TE.matmul(bank(6, 2), onesr[0:1, :], asmb[0:1, 148:150], start=True, stop=True)
            p_rep = c_pe.bump(TE.last)
            VE.wait_ge(c_pe.h, p_rep)
            VE.tensor_copy(SUP, bank(6, 2)[:, 0:1])
            VE.tensor_copy(SDN, bank(6, 2)[:, 1:2])
            vsp()
            VE.reciprocal(INVSUP, SUP)
            VE.reciprocal(INVSDN, SDN)
            vsp()
            f_scales = c_dve.bump(VE.last)
            b6_cons = f_scales

        def emit_tern(units):
            while units > 0 and t_cur[0] < len(tern_jobs):
                kind, k = tern_jobs[t_cur[0]]
                if kind == "up":
                    if k >= 2:
                        GP.wait_ge(c_dve.h, tern_up[k - 2])
                    m_wup[k] = dsTU.dma(wtf[:, (k % 2) * 4096:(k % 2 + 1) * 4096],
                                        wupT[128 * k:128 * (k + 1), :])
                    if k == 0:
                        VE.wait_ge(c_dve.h, f_scales)
                    dsTU.wait(VE, m_wup[k])
                    tb = tbo if k % 2 == 0 else tbo2
                    if k >= 2:
                        dsTO.wait(VE, to_tok[k - 2])
                    VE.tensor_scalar(out=ti32[:, :], in0=wtf[:, (k % 2) * 4096:(k % 2 + 1) * 4096],
                                     scalar1=INVSUP, scalar2=None, op0=ALU.mult)
                    VE.tensor_scalar(out=tb[:, :], in0=ti32[:, :], scalar1=1.0, scalar2=-1.0,
                                     op0=ALU.min, op1=ALU.max)
                    tern_up[k] = c_dve.bump(VE.last)
                    GP.wait_ge(c_dve.h, tern_up[k])
                    to_tok[k] = dsTO.dma(tup_h[128 * k:128 * (k + 1), :], tb[:, :])
                    units -= 2
                else:
                    if k >= 2:
                        GP.wait_ge(c_dve.h, tern_dn[k - 2])
                    elif k == 0:
                        GP.wait_ge(c_dve.h, tern_up[NT - 1])
                    m_wdn[k] = dsTD.dma(wtf[:, (k % 2) * 2048:(k % 2) * 2048 + 2048],
                                        wdnT[128 * k:128 * (k + 1), :])
                    dsTD.wait(VE, m_wdn[k])
                    tb = tbo if k % 2 == 0 else tbo2
                    dsTO.wait(VE, to_tok[NT + k - 2] if k >= 2 else to_tok[NT - 2 + k])
                    VE.tensor_scalar(out=ti32[:, 0:2048],
                                     in0=wtf[:, (k % 2) * 2048:(k % 2) * 2048 + 2048],
                                     scalar1=INVSDN, scalar2=None, op0=ALU.mult)
                    VE.tensor_scalar(out=tb[:, 0:2048], in0=ti32[:, 0:2048], scalar1=1.0,
                                     scalar2=-1.0, op0=ALU.min, op1=ALU.max)
                    tern_dn[k] = c_dve.bump(VE.last)
                    GP.wait_ge(c_dve.h, tern_dn[k])
                    to_tok[NT + k] = dsTO.dma(tdn_h[128 * k:128 * (k + 1), :], tb[:, 0:2048])
                    units -= 1
                t_cur[0] += 1

        sxA = [0]
        sxB = [0]
        sxC = [0]
        xqd = [0] * 8
        f_st = [0] * 8
        f_sq = [0] * 8
        a_r = [0] * 8
        m_xs = [0] * 8
        xq_pt = [0] * 8
        p7_cons = [0]

        def emit_statA():
            if sxA[0] >= 8:
                return
            qt = sxA[0]
            if qt == 0:
                SY.wait_ge(c_dve.h, tern_dn[ITL - 1])
            elif qt >= 2:
                SY.wait_ge(c_dve.h, f_sq[qt - 2])
            m_xs[qt] = dsXS.dma(xst[:, (qt % 2) * D:(qt % 2 + 1) * D],
                                x_sl[128 * qt:128 * (qt + 1), :])
            dsXS.wait(VE, m_xs[qt])
            xt_ = xst[:, (qt % 2) * D:(qt % 2 + 1) * D]
            for j in range(4):
                VE.bn_stats(out=asmb[:, 6 * j:6 * (j + 1)],
                            in_=xt_[:, 512 * j:512 * (j + 1)])
            vsp()
            VE.bn_aggr(out=asmb[:, 24:26], in_=asmb[:, 0:24])
            vsp()
            VE.tensor_copy(tmv[:, 0 + qt:1 + qt], asmb[:, 24:25])
            VE.tensor_copy(tmv[:, 8 + qt:9 + qt], asmb[:, 25:26])
            VE.tensor_reduce(out=sm2[:, 48 + qt:49 + qt], in_=xt_, axis=AX.X, op=ALU.max)
            VE.tensor_reduce(out=sm2[:, 56 + qt:57 + qt], in_=xt_, axis=AX.X, op=ALU.min)
            vsp(2)
            f_st[qt] = c_dve.bump(VE.last)
            sxA[0] += 1

        def emit_statB():
            if sxB[0] >= sxA[0] or sxB[0] >= 8:
                return
            qt = sxB[0]
            xt_ = xst[:, (qt % 2) * D:(qt % 2 + 1) * D]
            SC.wait_ge(c_dve.h, f_st[qt])
            SC.activation(out=tmv[:, 8 + qt:9 + qt], in_=tmv[:, 8 + qt:9 + qt],
                          func=AF.Ln, bias=EPSC, scale=1.0)
            SC.copy(out=sgs[:, 0:8], in_=small[:, 0:8])
            SC.copy(out=sgs[:, 0:8], in_=small[:, 0:8])
            SC.activation(out=tmv[:, 16 + qt:17 + qt], in_=tmv[:, 8 + qt:9 + qt],
                          func=AF.Exp, bias=ZEROC, scale=-0.5)
            SC.copy(out=sgs[:, 0:8], in_=small[:, 0:8])
            SC.copy(out=sgs[:, 0:8], in_=small[:, 0:8])
            a_r[qt] = c_act.bump(SC.last)
            VE.wait_ge(c_act.h, a_r[qt])
            VE.tensor_tensor(out=sm2[:, 32 + qt:33 + qt], in0=sm2[:, 48 + qt:49 + qt],
                             in1=tmv[:, 0 + qt:1 + qt], op=ALU.subtract)
            VE.tensor_tensor(out=sm2[:, 40 + qt:41 + qt], in0=tmv[:, 0 + qt:1 + qt],
                             in1=sm2[:, 56 + qt:57 + qt], op=ALU.subtract)
            vsp()
            VE.tensor_tensor(out=tmv[:, 24 + qt:25 + qt], in0=sm2[:, 32 + qt:33 + qt],
                             in1=sm2[:, 40 + qt:41 + qt], op=ALU.max)
            vsp()
            VE.tensor_tensor(out=tmv[:, 24 + qt:25 + qt], in0=tmv[:, 24 + qt:25 + qt],
                             in1=tmv[:, 16 + qt:17 + qt], op=ALU.mult)
            vsp()
            VE.tensor_scalar(out=tmv[:, 32 + qt:33 + qt], in0=tmv[:, 24 + qt:25 + qt],
                             scalar1=EPS, scalar2=None, op0=ALU.max)
            vsp()
            VE.reciprocal(sm2[:, 32 + qt:33 + qt], tmv[:, 32 + qt:33 + qt])
            vsp()
            VE.tensor_tensor(out=sm2[:, 40 + qt:41 + qt], in0=sm2[:, 32 + qt:33 + qt],
                             in1=tmv[:, 16 + qt:17 + qt], op=ALU.mult)
            vsp()
            VE.tensor_scalar(out=tmv[:, 40 + qt:41 + qt], in0=sm2[:, 40 + qt:41 + qt],
                             scalar1=127.0, scalar2=None, op0=ALU.mult)
            vsp()
            VE.tensor_scalar(out=sqi[:, :], in0=xt_,
                             scalar1=tmv[:, 0 + qt:1 + qt], scalar2=tmv[:, 40 + qt:41 + qt],
                             op0=ALU.subtract, op1=ALU.mult)
            if qt >= 2:
                VE.wait_ge(c_pe.h, xq_pt[qt - 2])
            VE.tensor_copy(sqb[:, (qt % 2) * D:(qt % 2 + 1) * D], sqi[:, :])
            f_sq[qt] = c_dve.bump(VE.last)
            sxB[0] += 1

        def emit_statC():
            if sxC[0] >= sxB[0] or sxC[0] >= 8:
                return
            qt = sxC[0]
            for g in range(2):
                TE.wait_ge(c_dve.h, f_sq[qt] if g == 0 else p7_cons[0])
                if g == 0:
                    TE.wait_ge(c_dve.h, p7_cons[0] if p7_cons[0] else vtr)
                for j in range(8):
                    k = 8 * g + j
                    TE.transpose(psb7[:, 128 * j:128 * (j + 1)],
                                 sqb[:, (qt % 2) * D + 128 * k:(qt % 2) * D + 128 * (k + 1)],
                                 identb[:, :])
                p_t = c_pe.bump(TE.last)
                xq_pt[qt] = p_t
                VE.wait_ge(c_pe.h, p_t)
                VE.tensor_copy(
                    xq[:, :].rearrange("p (k t) -> p k t", k=NT)[:, 8 * g:8 * g + 8,
                                                                 128 * qt:128 * (qt + 1)],
                    psb7[:, :].rearrange("p (k t) -> p k t", k=8))
                p7_cons[0] = c_dve.bump(VE.last)
            xqd[qt] = p7_cons[0]
            sxC[0] += 1

        def slot_hook(h, p, sub):
            if h == 0:
                if p == 1 and sub == 2:
                    emit_scales()
                elif p >= 2 and f_scales:
                    emit_tern(2 if sub < 2 else 3)
                return
            if h < 3:
                if f_scales:
                    emit_tern(2 if sub < 2 else 3)
                return
            assert t_cur[0] == len(tern_jobs), t_cur[0]
            emit_statC()
            emit_statB()
            emit_statA()

        def emit_preamble(hh):
            nonlocal vtr, b6_cons
            for r in range(4):
                if not (hh == 0 and r == 0):
                    TE.wait_ge(c_dve.h, vtr)
                for j in range(8):
                    kt = 8 * r + j
                    TE.transpose(psb7[:, 128 * j:128 * (j + 1)],
                                 qT[:, N * hh + 128 * kt:N * hh + 128 * (kt + 1)], identb[:, :])
                p_tr = c_pe.bump(TE.last)
                VE.wait_ge(c_pe.h, p_tr)
                VE.tensor_copy(vt[:, 1024 * r:1024 * (r + 1)], psb7[:, :])
                vtr = c_dve.bump(VE.last)
            for r8 in range(8):
                VE.tensor_tensor(out=sgs[:, :], in0=vt[:, 512 * r8:512 * (r8 + 1)],
                                 in1=vt[:, 512 * r8:512 * (r8 + 1)], op=ALU.mult)
                VE.tensor_reduce(out=asmb[:, 4 * r8:4 * (r8 + 1)],
                                 in_=sgs[:, :].rearrange("p (k e) -> p k e", k=4),
                                 axis=AX.X, op=ALU.add)
            vsp()
            VE.tensor_reduce(out=asmb[:, 32:33], in_=asmb[:, 0:32], axis=AX.X, op=ALU.max)
            vsp()
            f_nq = c_dve.bump(VE.last)
            TE.wait_ge(c_dve.h, f_nq)
            if b6_cons:
                TE.wait_ge(c_dve.h, b6_cons)
            TE.transpose(bank(6, 128)[0:1, :], asmb[:, 32:33], identf[:, :])
            p_nq = c_pe.bump(TE.last)
            VE.wait_ge(c_pe.h, p_nq)
            VE.tensor_copy(asmb[0:1, 0:128], bank(6, 128)[0:1, :])
            VE.tensor_reduce(out=asmb[0:1, 128:129], in_=asmb[0:1, 0:128], axis=AX.X, op=ALU.max)
            vsp()
            VE.tensor_scalar(out=asmb[0:1, 129:130], in0=asmb[0:1, 128:129],
                             scalar1=-S_ATT, scalar2=None, op0=ALU.mult)
            vsp()
            f_c = c_dve.bump(VE.last)
            b6_cons = f_c
            TE.wait_ge(c_dve.h, f_c)
            TE.matmul(bank(6, 1), onesr[0:1, :], asmb[0:1, 129:130], start=True, stop=True)
            p_c = c_pe.bump(TE.last)
            VE.wait_ge(c_pe.h, p_c)
            VE.tensor_copy(BIASC[hh], bank(6, 1)[:, 0:1])
            vsp()
            f_ch_arr[hh] = c_dve.bump(VE.last)
            b6_cons = f_ch_arr[hh]

        for h in range(NHD):
            sac = sacc[:, 128 * (h % 2):128 * (h % 2) + 128]
            if h == 0:
                TE.wait_ge(c_dve.h, p1_dve)
                emit_preamble(0)
            SC.wait_ge(c_dve.h, f_ch_arr[h])
            if h >= 2:
                SC.wait_ge(c_dve.h, sig_red[h - 2])

            for p in range(4):
                exp_kt = [0] * KT

                def emit_y(kt, exp_kt=exp_kt, p=p):
                    par = kt % 2
                    TE.wait_ge(c_act.h, exp_kt[kt])
                    if kt == 0 and y_evac:
                        TE.wait_ge(c_dve.h, y_evac)
                    for j in range(2):
                        TE.matmul(ps[:, 2048 + 512 * j:2048 + 512 * (j + 1)],
                                  vt[:, 128 * kt:128 * (kt + 1)],
                                  pt[:, 1024 * par + 512 * j:1024 * par + 512 * (j + 1)],
                                  start=(kt == 0), stop=(kt == KT - 1))
                    y_last[par] = c_pe.bump(TE.last)

                for kt in range(KT):
                    par = kt % 2
                    sb = 1024 * par
                    for j in range(2):
                        cq = 2 * p + j
                        TE.matmul(ps[:, sb + 512 * j:sb + 512 * (j + 1)],
                                  qT[:, N * h + 128 * kt:N * h + 128 * (kt + 1)],
                                  qT[:, N * h + 512 * cq:N * h + 512 * (cq + 1)],
                                  start=True, stop=True)
                    sc_mk = c_pe.bump(TE.last)
                    SC.wait_ge(c_pe.h, sc_mk)
                    SC.activation(out=pt[:, 1024 * par:1024 * (par + 1)],
                                  in_=ps[:, sb:sb + 1024], func=AF.Exp,
                                  scale=S_ATT, bias=BIASC[h],
                                  accum_out=sac[:, 4 * kt + p:4 * kt + p + 1])
                    exp_kt[kt] = c_act.bump(SC.last)
                    if kt >= 1:
                        emit_y(kt - 1)
                    if p == 0 and pending:
                        fn = pending.pop(kt, None)
                        if fn is not None:
                            fn()
                    k0, k1 = (22, 27) if p == 0 else (10, 21)
                    if kt == k0 or kt == k1:
                        slot_hook(h, p, 0 if kt == k0 else 1)
                emit_y(KT - 1)
                VE.wait_ge(c_pe.h, y_last[(KT - 1) % 2])
                VE.tensor_copy(yraw[:, 1024 * p:1024 * p + 512], bank(4))
                VE.tensor_copy(yraw[:, 1024 * p + 512:1024 * (p + 1)], bank(5))
                y_evac = c_dve.bump(VE.last)
                slot_hook(h, p, 2)
            for _ in range(4):
                SC.copy(out=sgs[:, 0:8], in_=small[:, 0:8])
            a_sig = c_act.bump(SC.last)

            tl = {"rb": [0] * NCH}

            def tail_sigma(h=h, sac=sac, a_sig=a_sig, tl=tl):
                if h > 0:
                    dsSW.wait(VE, ms1[h - 1])
                VE.wait_ge(c_act.h, a_sig)
                VE.tensor_reduce(out=scol[:, :], in_=sac.rearrange("p (k e) -> p k e", k=KT),
                                 axis=AX.X, op=ALU.add)
                vsp()
                sig_red[h] = c_dve.bump(VE.last)
                VE.reciprocal(rcol[:, :], scol[:, :])
                vsp()
                tl["f_rc"] = c_dve.bump(VE.last)

            def tail_rcolT(h=h, tl=tl):
                nonlocal b6_cons
                TE.wait_ge(c_dve.h, tl["f_rc"])
                if b6_cons:
                    TE.wait_ge(c_dve.h, b6_cons)
                TE.transpose(bank(6, 128)[0:32, :], rcol[:, :], identf[:, :])
                p_rt = c_pe.bump(TE.last)
                VE.wait_ge(c_pe.h, p_rt)
                VE.tensor_copy(fsc[0:32, 0:128], bank(6, 128)[0:32, :])
                vsp(2)
                tl["f_rt"] = c_dve.bump(VE.last)
                b6_cons = tl["f_rt"]

            def tail_swz(h=h, tl=tl):
                SY.wait_ge(c_dve.h, tl["f_rt"])
                if h > 0:
                    SY.wait_ge(c_pe.h, prb_last[h - 1])
                if h == 0 and dbg.get("fst") is not None:
                    SC.wait_ge(c_dve.h, tl["f_rt"])
                    dsDBG.dma(dbg["fst"][:, :], fsc[0:32, 0:128])
                ms1[h] = dsSW.dma(rowq[0:1, :], fsc[0:32, 0:128])
                if h == 0 and dbg.get("swz") is not None:
                    dsSW.wait(SY, ms1[h])
                    dsDBG.dma(dbg["swz"][0:1, :], rowq[0:1, :])

            def tail_rb(c, h=h, tl=tl):
                nonlocal b6_cons
                if c == 0:
                    dsSW.wait(TE, ms1[h])
                    TE.wait_ge(c_dve.h, b6_cons)
                else:
                    TE.wait_ge(c_dve.h, tl["rb"][c - 1])
                TE.matmul(bank(6), onesr[0:1, :], rowq[0:1, 512 * c:512 * (c + 1)],
                          start=True, stop=True)
                p_rb = c_pe.bump(TE.last)
                if c == NCH - 1:
                    prb_last[h] = p_rb
                VE.wait_ge(c_pe.h, p_rb)
                VE.tensor_copy(bcf[:, 512 * c:512 * (c + 1)], bank(6))
                tl["rb"][c] = c_dve.bump(VE.last)
                if c == NCH - 1:
                    b6_cons = tl["rb"][c]

            def tail_pub(h=h, tl=tl):
                nonlocal f_yn_prev
                if h == 0 and dbg.get("bcf") is not None:
                    SC.wait_ge(c_dve.h, tl["rb"][NCH - 1])
                    dsDBG.dma(dbg["bcf"][:, :], bcf[:, :])
                    dsDBG.dma(dbg["rowq"][0:1, :], rowq[0:1, :])
                    dsDBG.dma(dbg["rcol"][:, :], rcol[:, :])
                if h > 0:
                    dsPUB.wait(VE, pub[h - 1])
                VE.tensor_tensor(out=ynb[:, :], in0=yraw[:, :], in1=bcf[:, :], op=ALU.mult)
                f_yn = c_dve.bump(VE.last)
                f_yn_prev = f_yn
                GP.wait_ge(c_dve.h, f_yn)
                pub[h] = dsPUB.dma(
                    cc2i[128 * N * h:128 * N * (h + 1)].rearrange("(p t) -> p t", p=128),
                    ynb[:, :])
                if dbg.get("yn") is not None:
                    SC.wait_ge(c_dve.h, f_yn)
                    dsDBG.dma(dbg["yn"][128 * h:128 * (h + 1), :], ynb[:, :])
                    dsDBG.dma(dbg["sig"][128 * h:128 * (h + 1), :], scol[:, :])

            if h + 1 < NHD:
                emit_preamble(h + 1)
                pending = {2: tail_sigma, 6: tail_rcolT, 8: tail_swz, 19: tail_pub}
                for c in range(NCH):
                    pending[10 + c] = (lambda c=c: tail_rb(c))
            else:
                tail_sigma()
                tail_rcolT()
                tail_swz()
                for c in range(NCH):
                    tail_rb(c)
                tail_pub()

        emit_tern(99)
        while sxA[0] < 8:
            emit_statA()
        while sxB[0] < 8:
            emit_statB()
        while sxC[0] < 8:
            emit_statC()

        if dbg.get("sml") is not None:
            SC.wait_ge(c_dve.h, c_dve.n)
            dsDBG.dma(dbg["sml"][:, 0:32], small[:, :])
            dsDBG.dma(dbg["sml"][:, 32:192], asmb[:, :])
        dsPUB.wait(GP, pub[NHD - 1])
        cc2d = c_gp.bump(GP.collective_compute(
            "AllGather", ALU.bypass, replica_groups=[[0, 1, 2, 3], [4, 5, 6, 7]],
            ins=[cc2i[:]], outs=[cc2o[:]]))

        if phases < 3:
            GP.wait_ge(c_gp.h, cc2d)
            drain_all()
            return nc, dbg

        f_xq = xqd[7]
        if dbg.get("xq") is not None:
            SC.wait_ge(c_dve.h, f_xq)
            dsDBG.dma(dbg["xq"][:, :], xq[:, :])

        def row_bcast(src_off, dstbuf):
            TE.wait_ge(c_dve.h, c_dve.n)
            TE.transpose(bank(6, 128)[0:8, :], tmv[:, src_off:src_off + 8], identf[:, :])
            p1 = c_pe.bump(TE.last)
            VE.wait_ge(c_pe.h, p1)
            VE.tensor_copy(asmb[0:8, 0:128], bank(6, 128)[0:8, :])
            vsp(2)
            f1 = c_dve.bump(VE.last)
            SY.wait_ge(c_dve.h, f1)
            s2 = dsSW.dma(rowq[0:1, 0:1024], asmb[0:8, 0:128])
            dsSW.wait(TE, s2)
            for cc in range(2):
                TE.matmul(bank(2 + cc), onesr[0:1, :], rowq[0:1, 512 * cc:512 * (cc + 1)],
                          start=True, stop=True)
            p2 = c_pe.bump(TE.last)
            SC.wait_ge(c_pe.h, p2)
            SC.copy(out=dstbuf[:, 0:512], in_=bank(2))
            SC.copy(out=dstbuf[:, 512:1024], in_=bank(3))
            return c_act.bump(SC.last)

        m_up = [0] * 8
        upld = [0] * 8
        upev = {}
        dsSW.wait(VE, ms1[NHD - 1])
        VE.memset(osb[:, :], 0.0)
        VE.memset(fsc[:, :], 0.0)
        c_dve.bump(VE.last)
        SY.wait_ge(c_pe.h, prb_last[NHD - 1])
        for sl in range(8):
            if sl >= 2:
                SY.wait_ge(c_pe.h, upld[sl - 2])
            sb = (sl % 2) * NT * 512
            m_up[sl] = dsUP.dma(
                upsl[:, sb:sb + NT * 512].rearrange("p (k i) -> p k i", k=NT),
                tup_h[:, 512 * sl:512 * (sl + 1)].rearrange("(k p) i -> p k i", p=128))
            if sl == 0:
                dsTO.drain(TE)
            dsUP.wait(TE, m_up[sl])
            for ch in range(2):
                for it4 in range(4):
                    it = 4 * sl + it4
                    bkt = 2 * (it4 % 2) + ch
                    key = (it4 % 2, ch)
                    if upev.get(key):
                        TE.wait_ge(c_dve.h, upev[key])
                    if sl == 0 and it4 == 0:
                        TE.wait_ge(c_dve.h, xqd[3] if ch == 0 else xqd[7])
                    for k in range(NT):
                        TE.matmul(bank(bkt),
                                  upsl[:, sb + 512 * k + 128 * it4:sb + 512 * k + 128 * (it4 + 1)],
                                  xq[:, T * k + 512 * ch:T * k + 512 * (ch + 1)],
                                  start=(k == 0), stop=(k == NT - 1))
                    pmk = c_pe.bump(TE.last)
                    if it4 == 3 and ch == 1:
                        upld[sl] = pmk
                    SC.wait_ge(c_pe.h, pmk)
                    SC.activation(out=sgs[:, :], in_=bank(bkt), func=AF.Relu,
                                  bias=ZEROC, scale=1.0)
                    a_rl = c_act.bump(SC.last)
                    VE.wait_ge(c_act.h, a_rl)
                    VE.tensor_tensor(out=v_sb[:, T * it + 512 * ch:T * it + 512 * (ch + 1)],
                                     in0=sgs[:, :], in1=sgs[:, :], op=ALU.mult)
                    vm = osb if ch == 0 else fsc
                    VE.tensor_tensor(out=vm[:, :], in0=vm[:, :],
                                     in1=v_sb[:, T * it + 512 * ch:T * it + 512 * (ch + 1)],
                                     op=ALU.max)
                    upev[key] = c_dve.bump(VE.last)
        f_up = c_dve.n

        TE.wait_ge(c_dve.h, f_up)
        for half, vm in enumerate((osb, fsc)):
            for j in range(4):
                TE.wait_ge(c_dve.h, c_dve.n)
                TE.transpose(bank(6, 128), vm[:, 128 * j:128 * (j + 1)], identf[:, :])
                pj = c_pe.bump(TE.last)
                VE.wait_ge(c_pe.h, pj)
                qq = 4 * half + j
                VE.tensor_reduce(out=tmv[:, 48 + qq:49 + qq], in_=bank(6, 128)[:, 0:128],
                                 axis=AX.X, op=ALU.max)
                c_dve.bump(VE.last)
        vsp()
        c_dve.bump(VE.last)
        VE.tensor_scalar(out=asmb[:, 0:8], in0=tmv[:, 32:40], scalar1=SUP, scalar2=1.0 / 127.0,
                         op0=ALU.mult, op1=ALU.mult)
        vsp()
        VE.tensor_tensor(out=asmb[:, 0:8], in0=asmb[:, 0:8], in1=asmb[:, 0:8], op=ALU.mult)
        vsp()
        VE.tensor_tensor(out=asmb[:, 8:16], in0=asmb[:, 0:8], in1=tmv[:, 48:56], op=ALU.mult)
        vsp()
        VE.tensor_scalar(out=asmb[:, 8:16], in0=asmb[:, 8:16], scalar1=SDN, scalar2=1.0 / 127.0,
                         op0=ALU.mult, op1=ALU.mult)
        vsp()
        VE.tensor_copy(tmv[:, 56:64], asmb[:, 8:16])
        VE.tensor_scalar(out=asmb[:, 16:24], in0=tmv[:, 48:56], scalar1=EPS, scalar2=None,
                         op0=ALU.max)
        vsp()
        VE.reciprocal(asmb[:, 16:24], asmb[:, 16:24])
        vsp()
        VE.tensor_scalar(out=tmv[:, 48:56], in0=asmb[:, 16:24], scalar1=127.0, scalar2=None,
                         op0=ALU.mult)
        vsp()
        f_beta = c_dve.bump(VE.last)
        if dbg.get("tm") is not None:
            SC.wait_ge(c_dve.h, f_beta)
            dsDBG.dma(dbg["tm"][:, :], tmv[:, :])
        a_s2b = row_bcast(48, s2bb)

        hqe = [0] * ITL
        for it in range(ITL):
            if it == 0:
                VE.wait_ge(c_act.h, a_s2b)
            VE.tensor_tensor(out=xsi[:, :], in0=v_sb[:, T * it:T * (it + 1)],
                             in1=s2bb[:, 0:1024], op=ALU.mult)
            VE.tensor_copy(v_sb[:, T * it:T * (it + 1)], xsi[:, :])
            hqe[it] = c_dve.bump(VE.last)
        f_hq = c_dve.n
        if dbg.get("hq") is not None:
            SC.wait_ge(c_dve.h, f_hq)
            dsDBG.dma(dbg["hq"][:, :], v_sb[:, :])

        m_dn = [0] * 4
        dnld = [0] * 4
        dnev = {}
        for dc in range(4):
            if dc >= 1:
                SY.wait_ge(c_pe.h, dnld[dc - 1])
            else:
                SY.wait_ge(c_pe.h, upld[7])
            m_dn[dc] = dsDN.dma(
                dnsl[:, :].rearrange("p (i e) -> p i e", i=ITL),
                tdn_h[:, 512 * dc:512 * (dc + 1)].rearrange("(i p) e -> p i e", p=128))
            dsDN.wait(TE, m_dn[dc])
            for qt in range(8):
                bkt = 4 + (qt % 2)
                key = qt % 2
                if dnev.get(key):
                    TE.wait_ge(c_dve.h, dnev[key])
                for it in range(ITL):
                    if dc == 0 and qt == 0:
                        TE.wait_ge(c_dve.h, hqe[it])
                    TE.matmul(bank(bkt), v_sb[:, T * it + 128 * qt:T * it + 128 * (qt + 1)],
                              dnsl[:, 512 * it:512 * (it + 1)],
                              start=(it == 0), stop=(it == ITL - 1))
                pmk = c_pe.bump(TE.last)
                if qt == 7:
                    dnld[dc] = pmk
                VE.wait_ge(c_pe.h, pmk)
                VE.tensor_scalar(out=ffo[:, D * qt + 512 * dc:D * qt + 512 * (dc + 1)],
                                 in0=bank(bkt), scalar1=TM_BETA(qt), scalar2=None,
                                 op0=ALU.mult)
                dnev[key] = c_dve.bump(VE.last)
        f_dn = c_dve.n

        SY.wait_ge(c_gp.h, cc2d)
        for arm in nc.switch_core_id(engines=[SY._e], n=512):
            ts_arm = (arm.logical % 8) % 4
            src = cc2o[:].rearrange("(s h p t) -> p (s h) t", s=4, h=NHD, p=128)
            i_y = SY.dma_start(out=yallT[:, :].rearrange("p (g t) -> p g t", g=16),
                               in_=src[:, :, T * ts_arm:T * (ts_arm + 1)])
            i_y.then_inc(dsYA.sems[0], 16)
        dsYA.cnt[0] += 16
        m_yall = (0, dsYA.cnt[0])

        m_bo = dsC.dma(bob[0:1, :], bod[0:1, :])
        dsC.wait(TE, m_bo)
        TE.wait_ge(c_dve.h, f_up)
        bo_cp = [0] * 4
        for j in range(4):
            if j >= 2:
                TE.wait_ge(c_act.h, bo_cp[j - 2])
            TE.matmul(bank(j % 2), onesr[0:1, :], bob[0:1, 512 * j:512 * (j + 1)],
                      start=True, stop=True)
            pj = c_pe.bump(TE.last)
            SC.wait_ge(c_pe.h, pj)
            SC.copy(out=bob[:, 512 * j:512 * (j + 1)], in_=bank(j % 2))
            bo_cp[j] = c_act.bump(SC.last)
        a_bo = bo_cp[3]

        m_wo = [0] * 4
        wold = [0] * 4
        woev = {}
        st_marks = []
        dve_at_tile = []
        m_xt = {}
        GP.wait_ge(c_dve.h, f_dn)
        for dc in range(4):
            if dc >= 1:
                GP.wait_ge(c_pe.h, wold[dc - 1])
            m_wo[dc] = dsWO.dma(
                wob[:, :].rearrange("p (k e) -> p k e", k=NT),
                wod[:, 512 * dc:512 * (dc + 1)].rearrange("(k p) e -> p k e", p=128))
            dsWO.wait(TE, m_wo[dc])
            if dc == 0:
                dsYA.wait(TE, m_yall)
                TE.wait_ge(c_act.h, a_bo)
            for qt in range(8):
                ii = 8 * dc + qt
                slot = ii % 2
                if ii >= 2:
                    SY.wait_ge(c_dve.h, dve_at_tile[ii - 2])
                m_xt[ii] = dsXT.dma(
                    xss[:, 512 * slot:512 * (slot + 1)],
                    x_sl[128 * qt:128 * (qt + 1), 512 * dc:512 * (dc + 1)])
                bkt = 2 + (qt % 2)
                key = qt % 2
                if woev.get(key):
                    TE.wait_ge(c_dve.h, woev[key])
                for k in range(NT):
                    TE.matmul(bank(bkt), yallT[:, T * k + 128 * qt:T * k + 128 * (qt + 1)],
                              wob[:, 512 * k:512 * (k + 1)],
                              start=(k == 0), stop=(k == NT - 1))
                pmk = c_pe.bump(TE.last)
                if qt == 7:
                    wold[dc] = pmk
                ob = osb if ii % 2 == 0 else fsc
                VE.wait_ge(c_pe.h, pmk)
                dsXT.wait(VE, m_xt[ii])
                if len(st_marks) >= 2:
                    dsST.wait(VE, st_marks[-2])
                VE.scalar_tensor_tensor(out=ob[:, :], in0=bank(bkt), scalar=1.0,
                                        in1=bob[:, 512 * dc:512 * (dc + 1)],
                                        op0=ALU.mult, op1=ALU.add)
                VE.tensor_tensor(out=ob[:, :], in0=ob[:, :],
                                 in1=ffo[:, D * qt + 512 * dc:D * qt + 512 * (dc + 1)],
                                 op=ALU.add)
                VE.tensor_tensor(out=ob[:, :], in0=ob[:, :],
                                 in1=xss[:, 512 * slot:512 * (slot + 1)], op=ALU.add)
                woev[key] = c_dve.bump(VE.last)
                dve_at_tile.append(woev[key])
                SC.wait_ge(c_dve.h, woev[key])
                st_marks.append(dsST.dma(
                    outd[128 * qt:128 * (qt + 1), 512 * dc:512 * (dc + 1)], ob[:, :]))

        drain_all()

    return nc, dbg


_CACHE = {}


def _build(phases=3, debug=0):
    key = (phases, debug)
    if key not in _CACHE:
        _CACHE[key] = build_nc(phases=phases, debug=debug)
    return _CACHE[key]


def make_in_maps(x, Wq, Wo, bo, w_up, w_dn):
    wupT = np.ascontiguousarray(w_up.T)
    wdnT = np.ascontiguousarray(w_dn.T)
    idb = np.eye(128, dtype=BF)
    idf = np.eye(128, dtype=np.float32)
    bo2 = np.ascontiguousarray(bo.reshape(1, D))
    maps = []
    for c in range(8):
        b, hg, ts = c // 4, c % 4, c % 4
        xTb = np.ascontiguousarray(x[b].T)
        maps.append({
            "xT": xTb,
            "xslT": np.ascontiguousarray(xTb[:, T * ts:T * (ts + 1)]),
            "x_sl": np.ascontiguousarray(x[b, T * ts:T * (ts + 1), :]),
            "wqd": np.ascontiguousarray(Wq[:, 512 * hg:512 * (hg + 1)]),
            "wod": Wo,
            "wupT": wupT,
            "wdnT": wdnT,
            "bod": bo2,
            "wshu": np.ascontiguousarray(wupT[256 * c:256 * (c + 1), :]),
            "wshd": np.ascontiguousarray(wdnT[512 * c:512 * (c + 1), :]),
            "idbd": idb,
            "idfd": idf,
        })
    return maps


def kernel(L0_attn_x, L0_attn_Wq, L0_attn_Wo, L0_attn_bo, L0_ffn_w_up, L0_ffn_w_down):
    x = np.asarray(L0_attn_x, dtype=np.float32)
    maps = make_in_maps(x, np.asarray(L0_attn_Wq, np.float32),
                        np.asarray(L0_attn_Wo, np.float32),
                        np.asarray(L0_attn_bo, np.float32),
                        np.asarray(L0_ffn_w_up, np.float32),
                        np.asarray(L0_ffn_w_down, np.float32))
    nc, _ = _build(3, 0)
    res = run_bass_kernel_spmd(nc, maps, core_ids=list(range(8)))
    o = np.empty((B, N, D), dtype=np.float32)
    for c in range(8):
        b, ts = c // 4, c % 4
        o[b, T * ts:T * (ts + 1), :] = res.results[c]["outd"]
    return o


# version 12
# speedup vs baseline: 1.0934x; 1.0934x over previous
import numpy as np
import ml_dtypes
import concourse.bass as bass
import concourse.mybir as mybir
from concourse.bass_utils import run_bass_kernel_spmd

F32 = mybir.dt.float32
BF16 = mybir.dt.bfloat16
I32 = mybir.dt.int32
AF = mybir.ActivationFunctionType
ALU = mybir.AluOpType
AX = mybir.AxisListType

B, N, D = 2, 4096, 2048
T = 1024
I = 4096
NT = D // 128
NCH = N // 512
KT = N // 128
ITL = I // 128
NHD = 4
S_ATT = float(128.0 ** -0.5)
EPS = 1e-5
BF = ml_dtypes.bfloat16
GS = [3] * 10 + [2]


class Ctr:
    def __init__(self, nc, name):
        self.h = nc.alloc_semaphore(name=name)
        self.n = 0
        self._seen = []

    def bump(self, instr, k=1):
        assert id(instr) not in {id(o) for o in self._seen}, "double then_inc"
        self._seen.append(instr)
        instr.then_inc(self.h, k)
        self.n += k
        return self.n


class DStream:

    def __init__(self, nc, eng, name, nbuf=2):
        self.eng = eng
        self.sems = [nc.alloc_semaphore(name=f"{name}{i}") for i in range(nbuf)]
        self.cnt = [0] * nbuf
        self.i = 0

    def dma(self, out, in_):
        j = self.i % len(self.sems)
        instr = self.eng.dma_start(out=out, in_=in_)
        instr.then_inc(self.sems[j], 16)
        self.cnt[j] += 16
        self.i += 1
        return (j, self.cnt[j])

    def wait(self, weng, tok):
        j, v = tok
        weng.wait_ge(self.sems[j], v)

    def drain(self, weng):
        for j in range(len(self.sems)):
            if self.cnt[j]:
                weng.wait_ge(self.sems[j], self.cnt[j])


class Eng:

    _NOTRACK = {"wait_ge", "sem_clear", "last"}

    def __init__(self, eng):
        object.__setattr__(self, "_e", eng)
        object.__setattr__(self, "last", None)

    def __getattr__(self, n):
        f = getattr(self._e, n)
        if not callable(f) or n in self._NOTRACK or n.startswith("_"):
            return f
        def wrap(*a, **k):
            r = f(*a, **k)
            if hasattr(r, "then_inc"):
                object.__setattr__(self, "last", r)
            return r
        return wrap


def build_nc(phases=3, debug=0, stop=None):
    nc = bass.Bass("TRN2", target_bir_lowering=False, detect_race_conditions=False)

    xT = nc.dram_tensor("xT", [D, N], F32, kind="ExternalInput").ap()
    xslT = nc.dram_tensor("xslT", [D, T], F32, kind="ExternalInput").ap()
    x_sl = nc.dram_tensor("x_sl", [T, D], F32, kind="ExternalInput").ap()
    wqd = nc.dram_tensor("wqd", [D, 512], F32, kind="ExternalInput").ap()
    wod = nc.dram_tensor("wod", [D, D], F32, kind="ExternalInput").ap()
    wupT = nc.dram_tensor("wupT", [D, I], F32, kind="ExternalInput").ap()
    wdnT = nc.dram_tensor("wdnT", [I, D], F32, kind="ExternalInput").ap()
    bod = nc.dram_tensor("bod", [1, D], F32, kind="ExternalInput").ap()
    wshu = nc.dram_tensor("wshu", [256, I], F32, kind="ExternalInput").ap()
    wshd = nc.dram_tensor("wshd", [512, D], F32, kind="ExternalInput").ap()
    idbd = nc.dram_tensor("idbd", [128, 128], BF16, kind="ExternalInput").ap()
    idfd = nc.dram_tensor("idfd", [128, 128], F32, kind="ExternalInput").ap()
    outd = nc.dram_tensor("outd", [T, D], F32, kind="ExternalOutput").ap()

    tup_h = nc.dram_tensor("tup_h", [D, I], BF16, kind="Internal").ap()
    tdn_h = nc.dram_tensor("tdn_h", [I, D], BF16, kind="Internal").ap()
    cc1i = nc.dram_tensor("cc1i", [1, 2], F32, kind="Internal").ap()
    cc1o = nc.dram_tensor("cc1o", [8, 2], F32, kind="Internal").ap()
    cc2i = nc.dram_tensor("cc2i", [NHD * 128 * N], BF16, kind="Internal").ap()
    cc2o = nc.dram_tensor("cc2o", [4 * NHD * 128 * N], BF16, kind="Internal").ap()
    swzd = nc.dram_tensor("swzd", [4096], F32, kind="Internal").ap()

    dbg = {}
    if debug >= 1:
        dbg["qT"] = nc.dram_tensor("dbg_qT", [128, NHD * N], BF16, kind="ExternalOutput").ap()
        if phases >= 2:
            dbg["yn"] = nc.dram_tensor("dbg_yn", [NHD * 128, N], BF16, kind="ExternalOutput").ap()
            dbg["sig"] = nc.dram_tensor("dbg_sig", [NHD * 128, KT], F32, kind="ExternalOutput").ap()
            dbg["sml"] = nc.dram_tensor("dbg_sml", [128, 32 + 160], F32, kind="ExternalOutput").ap()
            dbg["bcf"] = nc.dram_tensor("dbg_bcf", [128, N], F32, kind="ExternalOutput").ap()
            dbg["rowq"] = nc.dram_tensor("dbg_rowq", [1, N], F32, kind="ExternalOutput").ap()
            dbg["rcol"] = nc.dram_tensor("dbg_rcol", [128, 32], F32, kind="ExternalOutput").ap()
            dbg["fst"] = nc.dram_tensor("dbg_fst", [32, 128], F32, kind="ExternalOutput").ap()
            dbg["swz"] = nc.dram_tensor("dbg_swz", [1, 4096], F32, kind="ExternalOutput").ap()
        if phases >= 3:
            dbg["xq"] = nc.dram_tensor("dbg_xq", [128, NT * T], BF16, kind="ExternalOutput").ap()
            dbg["hq"] = nc.dram_tensor("dbg_hq", [128, ITL * T], BF16, kind="ExternalOutput").ap()
            dbg["tm"] = nc.dram_tensor("dbg_tm", [128, 64], F32, kind="ExternalOutput").ap()

    SBASE = 16512

    def at(name, shape, dtype, off):
        return nc.alloc_sbuf_tensor_at(name, shape, dtype, offset=SBASE + off)

    qT = at("qT", [128, NHD * N], BF16, 0x00000)
    upsl = at("upsl", [128, 2 * NT * 512], BF16, 0x00000)
    dnsl = at("dnsl", [128, ITL * 512], BF16, 0x00000)
    xb = at("xb", [128, 2 * NT * 512], BF16, 0x08000)
    xq = at("xq", [128, NT * T], BF16, 0x08000)
    ffo = at("ffo", [128, 8 * D], BF16, 0x08000)
    wqb = at("wqb", [128, NT * 512], BF16, 0x10000)
    xsf = at("xsf", [128, 2 * T], F32, 0x10000)
    xsi = at("xsi", [128, T], I32, 0x12000)
    wob = at("wob", [128, NT * 512], BF16, 0x10000)
    wob2 = at("wob2", [128, NT * 512], BF16, 0x14000)
    rowq = at("rowq", [1, N], F32, 0x10000)
    wtf = at("wtf", [128, 2 * 4096], F32, 0x14000)
    ti32 = at("ti32", [128, 4096], I32, 0x1C000)
    tbo = at("tbo", [128, 4096], BF16, 0x20000)
    tbo2 = at("tbo2", [128, 4096], BF16, 0x22000)
    x2w = at("x2w", [128, 8192], BF16, 0x20000)
    v_sb = at("v_sb", [128, ITL * T], BF16, 0x14000)
    bcf = at("bcf", [128, N], F32, 0x24000)
    yraw = at("yraw", [128, N], F32, 0x28000)
    yallT = at("yallT", [128, 16 * T], BF16, 0x24000)
    xss = at("xss", [128, 2 * 512], F32, 0x2C000)
    vt = at("vt", [128, N], BF16, 0x2C000)
    xst = at("xst", [128, 2 * D], F32, 0x14000)
    sqi = at("sqi", [128, D], I32, 0x1C000)
    sqb = at("sqb", [128, 2 * D], BF16, 0x20000)
    s2bb = at("s2bb", [128, 2 * 512], F32, 0x2E000)
    pt = at("pt", [128, 2 * 1536], BF16, 0x2E000)
    ynb = at("ynb", [128, N], BF16, 0x2F800)
    bob = at("bob", [128, D], F32, 0x2F800)
    x2s = at("x2s", [128, 512], F32, 0x1C000)
    x2t = at("x2t", [128, 512], F32, 0x1C800)
    rbc = at("rbc", [128, 512], F32, 0x2E000)
    mbc = at("mbc", [128, 512], F32, 0x2E800)
    rowc = at("rowc", [1, 512], F32, 0x2F000)
    wbr = at("wbr", [1, 512], F32, 0x2F800)
    osb = at("osb", [128, 512], F32, 0x31800)
    fsc = at("fsc", [128, 512], F32, 0x32000)
    sgs = at("sgs", [128, 512], BF16, 0x32800)
    scol = at("scol", [128, 32], F32, 0x32C00)
    rcol = at("rcol", [128, 32], F32, 0x32C80)
    asmb = at("asmb", [128, 160], F32, 0x32D00)
    sm2 = at("sm2", [128, 64], F32, 0x32F80)
    tmv = at("tmv", [128, 64], F32, 0x33080)
    small = at("small", [128, 32], F32, 0x33180)
    identb = at("identb", [128, 128], BF16, 0x33200)
    identf = at("identf", [128, 128], F32, 0x33300)
    onesb = at("onesb", [128, 1], BF16, 0x33500)
    onesr = at("onesr", [1, 128], F32, 0x33520)
    sacc = at("sacc", [128, 256], F32, 0x33720)
    assert SBASE + 0x33720 + 1024 < 229344

    ps = nc.alloc_psum_tensor("ps", [128, 3584], F32)
    psb7 = nc.alloc_psum_tensor("psb7", [128, 1024], BF16)

    def bank(i, cols=512):
        return ps[:, i * 512:i * 512 + cols]

    c_pe = Ctr(nc, "c_pe")
    c_act = Ctr(nc, "c_act")
    c_dve = Ctr(nc, "c_dve")
    c_gp = Ctr(nc, "c_gp")

    TE = Eng(nc.tensor)
    SC = Eng(nc.scalar)
    VE = Eng(nc.vector)
    GP = Eng(nc.gpsimd)
    SY = Eng(nc.sync)

    dsC = DStream(nc, nc.sync, "sC", 1)
    dsSH = DStream(nc, nc.sync, "sSH", 2)
    dsWQ = DStream(nc, nc.gpsimd, "sWQ", 1)
    dsXB = DStream(nc, nc.gpsimd, "sXB", 2)
    dsCC = DStream(nc, nc.gpsimd, "sCC", 1)
    dsSW = DStream(nc, nc.sync, "sSW", 1)
    dsPUB = DStream(nc, nc.gpsimd, "sPUB", 1)
    dsTU = DStream(nc, nc.gpsimd, "sTU", 2)
    dsTD = DStream(nc, nc.gpsimd, "sTD", 2)
    dsTO = DStream(nc, nc.gpsimd, "sTO", 2)
    dsXS = DStream(nc, nc.sync, "sXS", 2)
    dsXF = DStream(nc, nc.sync, "sXF", 2)
    dsUP = DStream(nc, nc.sync, "sUP", 2)
    dsUPV = DStream(nc, nc.scalar, "sUPV", 2)
    dsDN = DStream(nc, nc.sync, "sDN", 1)
    dsDNV = DStream(nc, nc.scalar, "sDNV", 1)
    dsWOV = DStream(nc, nc.scalar, "sWOV", 2)
    dsYA = DStream(nc, nc.sync, "sYA", 1)
    dsWO = DStream(nc, nc.gpsimd, "sWO", 2)
    dsXT = DStream(nc, nc.sync, "sXT", 2)
    dsST = DStream(nc, nc.scalar, "sST", 2)
    dsDBG = DStream(nc, nc.scalar, "sDBG", 1)
    ALL_DS = [dsC, dsSH, dsWQ, dsXB, dsCC, dsSW, dsPUB, dsTU, dsTD, dsTO,
              dsXS, dsXF, dsUP, dsUPV, dsDN, dsDNV, dsWOV, dsYA, dsWO, dsXT, dsST, dsDBG]

    ONESF = small[:, 0:1]
    EPSC = small[:, 1:2]
    ZEROC = small[:, 2:3]
    ACCFL = small[:, 3:4]
    BIASC = [small[:, 4 + i:5 + i] for i in range(NHD)]
    SUP = small[:, 8:9]
    SDN = small[:, 9:10]
    INVSUP = small[:, 10:11]
    INVSDN = small[:, 11:12]
    WCOL = [sm2[:, 16 + t:17 + t] for t in range(4)]
    TM_BETA = lambda q: tmv[:, 56 + q:57 + q]

    def vsp(n=4):
        for _ in range(n):
            VE.tensor_copy(sm2[:, 63:64], sm2[:, 62:63])

    def drain_all():
        for ds in ALL_DS:
            ds.drain(GP)
        GP.wait_ge(c_dve.h, c_dve.n)
        GP.wait_ge(c_pe.h, c_pe.n)
        GP.wait_ge(c_act.h, c_act.n)

    def fin(tag):
        if stop != tag:
            return False
        drain_all()
        return True

    with nc.Block() as block:
        dsC.dma(identb[:, :], idbd[:, :])
        t_id = dsC.dma(identf[:, :], idfd[:, :])
        t_shu = [dsSH.dma(wtf[:, 4096 * i:4096 * (i + 1)],
                          wshu[128 * i:128 * (i + 1), :]) for i in range(2)]
        VE.memset(ONESF, 1.0)
        VE.memset(EPSC, EPS)
        VE.memset(ZEROC, 0.0)
        VE.memset(onesb[:, :], 1.0)
        VE.memset(onesr[0:1, :], 1.0)
        f_const = c_dve.bump(VE.memset(scol[:, :], 0.0))
        SC.wait_ge(c_dve.h, f_const)
        SC.activation(out=sgs[:, 0:8], in_=small[:, 0:8], func=AF.Exp, scale=0.0,
                      bias=ZEROC, accum_out=ACCFL)
        dsSH.wait(VE, t_shu[0])
        VE.tensor_reduce(out=asmb[:, 150:151], in_=wtf[:, 0:4096], axis=AX.X,
                         op=ALU.add, apply_absolute_value=True)
        dsSH.wait(VE, t_shu[1])
        VE.tensor_reduce(out=asmb[:, 151:152], in_=wtf[:, 4096:8192], axis=AX.X,
                         op=ALU.add, apply_absolute_value=True)
        f_shu = c_dve.bump(VE.last)
        t_shd = [None] * 4
        sh_red = [0] * 4
        for i in range(4):
            if i == 0:
                SY.wait_ge(c_dve.h, f_shu)
            elif i >= 2:
                SY.wait_ge(c_dve.h, sh_red[i - 2])
            t_shd[i] = dsSH.dma(wtf[:, 2048 * i:2048 * (i + 1)],
                                wshd[128 * i:128 * (i + 1), :])
            dsSH.wait(VE, t_shd[i])
            VE.tensor_reduce(out=asmb[:, 152 + i:153 + i], in_=wtf[:, 2048 * i:2048 * (i + 1)],
                             axis=AX.X, op=ALU.add, apply_absolute_value=True)
            sh_red[i] = c_dve.bump(VE.last)
        VE.tensor_reduce(out=sm2[:, 0:1], in_=asmb[:, 150:152], axis=AX.X, op=ALU.add)
        VE.tensor_reduce(out=sm2[:, 1:2], in_=asmb[:, 152:156], axis=AX.X, op=ALU.add)
        vsp()
        f_shd = c_dve.bump(VE.last)
        if fin("ph0a"):
            return nc, dbg
        TE.wait_ge(c_dve.h, f_shd)
        dsC.wait(TE, t_id)
        TE.transpose(bank(6, 128)[0:2, :], sm2[:, 0:2], identf[:, :])
        p_t0 = c_pe.bump(TE.last)
        VE.wait_ge(c_pe.h, p_t0)
        VE.tensor_copy(asmb[0:2, 0:128], bank(6, 128)[0:2, :])
        VE.tensor_reduce(out=asmb[0:2, 128:129], in_=asmb[0:2, 0:128], axis=AX.X, op=ALU.add)
        vsp()
        f_par = c_dve.bump(VE.last)
        if fin("ph0b"):
            return nc, dbg

        t_wq = dsWQ.dma(wqb[:, :].rearrange("p (k c) -> p k c", k=NT),
                        wqd.rearrange("(k p) c -> p k c", p=128))
        dsWQ.wait(TE, t_wq)
        TE.wait_ge(c_dve.h, f_par)
        for t in range(4):
            for k in range(NT):
                TE.matmul(bank(2, 1), wqb[:, 512 * k + 128 * t:512 * k + 128 * (t + 1)],
                          onesb[:, :], start=(k == 0), stop=(k == NT - 1))
            pw = c_pe.bump(TE.last)
            VE.wait_ge(c_pe.h, pw)
            VE.tensor_copy(WCOL[t], bank(2, 1)[:, 0:1])
            vsp(2)
            wc_ve = c_dve.bump(VE.last)
            TE.wait_ge(c_dve.h, wc_ve)
        f_wcol = c_dve.n
        if fin("wbar"):
            return nc, dbg

        mx = [0] * NCH
        st_ev = [0] * NCH
        rows_rdy = [0] * NCH
        bc_mm = [0] * NCH
        qld = [0] * NCH
        x2mm = [0] * NCH
        q_ev = [0] * (4 * NCH)
        f_mr_list = [0] * NCH
        for c in range(NCH):
            half = (c % 2) * NT * 512

            def xbt(k, half=half):
                return xb[:, half + 512 * k:half + 512 * (k + 1)]

            if c >= 2:
                GP.wait_ge(c_pe.h, qld[c - 2])
            mx[c] = dsXB.dma(
                xb[:, half:half + NT * 512].rearrange("p (k t) -> p k t", k=NT),
                xT[:, 512 * c:512 * (c + 1)].rearrange("(k p) t -> p k t", p=128))

            dsXB.wait(TE, mx[c])
            if c >= 1:
                TE.wait_ge(c_act.h, st_ev[c - 1])
            for k in range(NT):
                TE.matmul(bank(0)[0:1, :], onesb[:, :], xbt(k), start=(k == 0), stop=(k == NT - 1))
            c_pe.bump(TE.last)
            dsXB.wait(VE, mx[c])
            if c >= 1:
                VE.wait_ge(c_pe.h, bc_mm[c - 1])
            if c >= 1:
                VE.wait_ge(c_pe.h, x2mm[c - 1])
            for j in range(4):
                VE.tensor_tensor(out=x2w[:, 2048 * j:2048 * (j + 1)],
                                 in0=xb[:, half + 2048 * j:half + 2048 * (j + 1)],
                                 in1=xb[:, half + 2048 * j:half + 2048 * (j + 1)],
                                 op=ALU.mult)
            f_x2 = c_dve.bump(VE.last)
            TE.wait_ge(c_dve.h, f_x2)
            for k in range(NT):
                TE.matmul(bank(1)[0:1, :], onesb[:, :], x2w[:, 512 * k:512 * (k + 1)],
                          start=(k == 0), stop=(k == NT - 1))
            x2mm[c] = c_pe.bump(TE.last)
            p_sxx = c_pe.n
            SC.wait_ge(c_pe.h, p_sxx)
            if c >= 1:
                SC.wait_ge(c_dve.h, q_ev[4 * (c - 1) + 3])
            SC.copy(out=rowc[0:1, :], in_=bank(0)[0:1, :])
            SC.copy(out=x2s[0:1, 0:512], in_=bank(1)[0:1, :])
            st_ev[c] = c_act.bump(SC.last)
            VE.wait_ge(c_act.h, st_ev[c])
            VE.tensor_scalar(out=rowc[0:1, :], in0=rowc[0:1, :], scalar1=1.0 / D,
                             scalar2=None, op0=ALU.mult)
            VE.tensor_scalar(out=x2s[0:1, 0:512], in0=x2s[0:1, 0:512], scalar1=1.0 / D,
                             scalar2=None, op0=ALU.mult)
            VE.tensor_tensor(out=x2t[0:1, 0:512], in0=rowc[0:1, :], in1=rowc[0:1, :], op=ALU.mult)
            VE.tensor_tensor(out=x2s[0:1, 0:512], in0=x2s[0:1, 0:512], in1=x2t[0:1, 0:512],
                             op=ALU.subtract)
            f_var = c_dve.bump(VE.last)
            SC.wait_ge(c_dve.h, f_var)
            SC.activation(out=x2s[0:1, 0:512], in_=x2s[0:1, 0:512], func=AF.Ln,
                          bias=EPSC[0:1, :], scale=1.0)
            SC.activation(out=x2s[0:1, 0:512], in_=x2s[0:1, 0:512], func=AF.Exp,
                          bias=ZEROC[0:1, :], scale=-0.5)
            a_r = c_act.bump(SC.last)
            VE.wait_ge(c_act.h, a_r)
            VE.tensor_tensor(out=x2t[0:1, 0:512], in0=rowc[0:1, :], in1=x2s[0:1, 0:512],
                             op=ALU.mult)
            VE.tensor_scalar(out=x2t[0:1, 0:512], in0=x2t[0:1, 0:512], scalar1=-1.0,
                             scalar2=None, op0=ALU.mult)
            f_mr_list[c] = c_dve.bump(VE.last)
            TE.wait_ge(c_dve.h, f_mr_list[c])
            TE.matmul(bank(5), onesr[0:1, :], x2s[0:1, 0:512], start=True, stop=True)
            TE.matmul(bank(6), onesr[0:1, :], x2t[0:1, 0:512], start=True, stop=True)
            bc_mm[c] = c_pe.bump(TE.last)
            SC.wait_ge(c_pe.h, bc_mm[c])
            SC.copy(out=rbc[:, :], in_=bank(5))
            SC.copy(out=mbc[:, :], in_=bank(6))
            rows_rdy[c] = c_act.bump(SC.last)

            for t in range(4):
                ii = 4 * c + t
                bq = bank(3 + (ii % 2))
                if ii >= 2:
                    TE.wait_ge(c_dve.h, q_ev[ii - 2])
                for k in range(NT):
                    TE.matmul(bq, wqb[:, 512 * k + 128 * t:512 * k + 128 * (t + 1)],
                              xbt(k), start=(k == 0), stop=(k == NT - 1))
                qp = c_pe.bump(TE.last)
                if t == 3:
                    qld[c] = qp
                VE.wait_ge(c_pe.h, qp)
                if t == 0:
                    VE.wait_ge(c_act.h, rows_rdy[c])
                VE.tensor_tensor(out=fsc[:, :], in0=bq, in1=rbc[:, :], op=ALU.mult)
                VE.scalar_tensor_tensor(out=qT[:, N * t + 512 * c:N * t + 512 * (c + 1)],
                                        in0=mbc[:, :], scalar=WCOL[t], in1=fsc[:, :],
                                        op0=ALU.mult, op1=ALU.add)
                q_ev[ii] = c_dve.bump(VE.last)
            if stop == "c" + str(c):
                break

        if stop is not None and stop.startswith("c") and fin(stop):
            return nc, dbg
        p1_dve = c_dve.n

        GP.wait_ge(c_dve.h, f_par)
        gm1 = dsCC.dma(cc1i[0:1, :].rearrange("a (b c) -> (a b) c", b=2),
                       asmb[0:2, 128:129])
        dsCC.wait(GP, gm1)
        cc1d = c_gp.bump(GP.collective_compute(
            "AllGather", ALU.bypass, replica_groups=[[0, 1, 2, 3, 4, 5, 6, 7]],
            ins=[cc1i[:, :]], outs=[cc1o[:, :]]))
        GP.wait_ge(c_gp.h, cc1d)
        gm2 = dsCC.dma(asmb[0:1, 132:148].rearrange("p (x y) -> p x y", x=8),
                       cc1o[:, :].rearrange("(u a) b -> u a b", u=1))

        if dbg.get("qT") is not None:
            SC.wait_ge(c_dve.h, p1_dve)
            dsDBG.dma(dbg["qT"][:, :], qT[:, :])

        if phases < 2:
            drain_all()
            return nc, dbg

        tern_up = [0] * NT
        tern_dn = [0] * ITL
        m_wup = [0] * NT
        m_wdn = [0] * ITL
        pub = [0] * NHD
        ms1 = [0] * NHD
        prb_last = [0] * NHD
        sig_red = [0] * NHD
        vtr = 0
        y_evac = 0
        b6_cons = 0
        f_yn_prev = 0
        y_last = {0: 0, 1: 0}
        f_scales = 0
        rb4_gate = 0
        f_ch_arr = [0] * NHD
        to_tok = [None] * (NT + ITL)
        tern_jobs = [("up", k) for k in range(NT)] + [("dn", k) for k in range(ITL)]
        t_cur = [0]

        def emit_scales():
            nonlocal f_scales, b6_cons
            dsCC.wait(VE, gm2)
            VE.tensor_reduce(out=asmb[0:1, 148:149], in_=asmb[0:1, 132:148:2],
                             axis=AX.X, op=ALU.add)
            VE.tensor_reduce(out=asmb[0:1, 149:150], in_=asmb[0:1, 133:148:2],
                             axis=AX.X, op=ALU.add)
            vsp()
            VE.tensor_scalar(out=asmb[0:1, 148:150], in0=asmb[0:1, 148:150],
                             scalar1=1.0 / (I * D), scalar2=EPS, op0=ALU.mult, op1=ALU.max)
            vsp()
            f_s = c_dve.bump(VE.last)
            TE.wait_ge(c_dve.h, f_s)
            TE.wait_ge(c_dve.h, b6_cons)
            TE.matmul(bank(6, 2), onesr[0:1, :], asmb[0:1, 148:150], start=True, stop=True)
            p_rep = c_pe.bump(TE.last)
            VE.wait_ge(c_pe.h, p_rep)
            VE.tensor_copy(SUP, bank(6, 2)[:, 0:1])
            VE.tensor_copy(SDN, bank(6, 2)[:, 1:2])
            vsp()
            VE.reciprocal(INVSUP, SUP)
            VE.reciprocal(INVSDN, SDN)
            vsp()
            f_scales = c_dve.bump(VE.last)
            b6_cons = f_scales

        tt_cur = [0]

        def tern_trigger(idx):
            kind, k = tern_jobs[idx]
            if kind == "up":
                if k >= 2:
                    GP.wait_ge(c_dve.h, tern_up[k - 2])
                m_wup[k] = dsTU.dma(wtf[:, (k % 2) * 4096:(k % 2 + 1) * 4096],
                                    wupT[128 * k:128 * (k + 1), :])
            else:
                if k >= 2:
                    GP.wait_ge(c_dve.h, tern_dn[k - 2])
                elif k == 0:
                    GP.wait_ge(c_dve.h, tern_up[NT - 2])
                m_wdn[k] = dsTD.dma(wtf[:, (k % 2) * 2048:(k % 2) * 2048 + 2048],
                                    wdnT[128 * k:128 * (k + 1), :])

        def emit_tern(units):
            while units > 0 and t_cur[0] < len(tern_jobs):
                while tt_cur[0] <= min(len(tern_jobs) - 1, t_cur[0] + 1):
                    tern_trigger(tt_cur[0])
                    tt_cur[0] += 1
                kind, k = tern_jobs[t_cur[0]]
                if kind == "up":
                    if k == 0:
                        VE.wait_ge(c_dve.h, f_scales)
                    dsTU.wait(VE, m_wup[k])
                    tb = tbo if k % 2 == 0 else tbo2
                    if k >= 2:
                        dsTO.wait(VE, to_tok[k - 2])
                    VE.tensor_scalar(out=ti32[:, :], in0=wtf[:, (k % 2) * 4096:(k % 2 + 1) * 4096],
                                     scalar1=INVSUP, scalar2=None, op0=ALU.mult)
                    VE.tensor_scalar(out=tb[:, :], in0=ti32[:, :], scalar1=1.0, scalar2=-1.0,
                                     op0=ALU.min, op1=ALU.max)
                    tern_up[k] = c_dve.bump(VE.last)
                    GP.wait_ge(c_dve.h, tern_up[k])
                    to_tok[k] = dsTO.dma(tup_h[128 * k:128 * (k + 1), :], tb[:, :])
                    units -= 2
                else:
                    dsTD.wait(VE, m_wdn[k])
                    tb = tbo if k % 2 == 0 else tbo2
                    dsTO.wait(VE, to_tok[NT + k - 2] if k >= 2 else to_tok[NT - 2 + k])
                    VE.tensor_scalar(out=ti32[:, 0:2048],
                                     in0=wtf[:, (k % 2) * 2048:(k % 2) * 2048 + 2048],
                                     scalar1=INVSDN, scalar2=None, op0=ALU.mult)
                    VE.tensor_scalar(out=tb[:, 0:2048], in0=ti32[:, 0:2048], scalar1=1.0,
                                     scalar2=-1.0, op0=ALU.min, op1=ALU.max)
                    tern_dn[k] = c_dve.bump(VE.last)
                    GP.wait_ge(c_dve.h, tern_dn[k])
                    to_tok[NT + k] = dsTO.dma(tdn_h[128 * k:128 * (k + 1), :], tb[:, 0:2048])
                    units -= 1
                t_cur[0] += 1

        sxA = [0]
        sxB = [0]
        sxC = [0]
        xqd = [0] * 8
        f_st = [0] * 8
        f_sq = [0] * 8
        a_r = [0] * 8
        m_xs = [0] * 8
        xq_pt = [0] * 8
        p7_cons = [0]

        sxT = [0]

        def statA_trig():
            if sxT[0] >= 8:
                return
            qt = sxT[0]
            if qt == 0:
                SY.wait_ge(c_dve.h, tern_dn[ITL - 1])
            elif qt >= 2:
                SY.wait_ge(c_dve.h, f_sq[qt - 2])
            m_xs[qt] = dsXS.dma(xst[:, (qt % 2) * D:(qt % 2 + 1) * D],
                                x_sl[128 * qt:128 * (qt + 1), :])
            sxT[0] += 1

        def emit_statA():
            if sxA[0] >= 8 or sxA[0] >= sxT[0]:
                return
            qt = sxA[0]
            dsXS.wait(VE, m_xs[qt])
            xt_ = xst[:, (qt % 2) * D:(qt % 2 + 1) * D]
            for j in range(4):
                VE.bn_stats(out=asmb[:, 6 * j:6 * (j + 1)],
                            in_=xt_[:, 512 * j:512 * (j + 1)])
            vsp()
            VE.bn_aggr(out=asmb[:, 24:26], in_=asmb[:, 0:24])
            vsp()
            VE.tensor_copy(tmv[:, 0 + qt:1 + qt], asmb[:, 24:25])
            VE.tensor_copy(tmv[:, 8 + qt:9 + qt], asmb[:, 25:26])
            VE.tensor_reduce(out=sm2[:, 48 + qt:49 + qt], in_=xt_, axis=AX.X, op=ALU.max)
            VE.tensor_reduce(out=sm2[:, 56 + qt:57 + qt], in_=xt_, axis=AX.X, op=ALU.min)
            vsp(2)
            f_st[qt] = c_dve.bump(VE.last)
            sxA[0] += 1

        def emit_statB():
            if sxB[0] >= sxA[0] or sxB[0] >= 8:
                return
            qt = sxB[0]
            xt_ = xst[:, (qt % 2) * D:(qt % 2 + 1) * D]
            SC.wait_ge(c_dve.h, f_st[qt])
            SC.activation(out=tmv[:, 8 + qt:9 + qt], in_=tmv[:, 8 + qt:9 + qt],
                          func=AF.Ln, bias=EPSC, scale=1.0)
            SC.copy(out=sgs[:, 0:8], in_=small[:, 0:8])
            SC.copy(out=sgs[:, 0:8], in_=small[:, 0:8])
            SC.activation(out=tmv[:, 16 + qt:17 + qt], in_=tmv[:, 8 + qt:9 + qt],
                          func=AF.Exp, bias=ZEROC, scale=-0.5)
            SC.copy(out=sgs[:, 0:8], in_=small[:, 0:8])
            SC.copy(out=sgs[:, 0:8], in_=small[:, 0:8])
            a_r[qt] = c_act.bump(SC.last)
            VE.wait_ge(c_act.h, a_r[qt])
            VE.tensor_tensor(out=sm2[:, 32 + qt:33 + qt], in0=sm2[:, 48 + qt:49 + qt],
                             in1=tmv[:, 0 + qt:1 + qt], op=ALU.subtract)
            VE.tensor_tensor(out=sm2[:, 40 + qt:41 + qt], in0=tmv[:, 0 + qt:1 + qt],
                             in1=sm2[:, 56 + qt:57 + qt], op=ALU.subtract)
            vsp()
            VE.tensor_tensor(out=tmv[:, 24 + qt:25 + qt], in0=sm2[:, 32 + qt:33 + qt],
                             in1=sm2[:, 40 + qt:41 + qt], op=ALU.max)
            vsp()
            VE.tensor_tensor(out=tmv[:, 24 + qt:25 + qt], in0=tmv[:, 24 + qt:25 + qt],
                             in1=tmv[:, 16 + qt:17 + qt], op=ALU.mult)
            vsp()
            VE.tensor_scalar(out=tmv[:, 32 + qt:33 + qt], in0=tmv[:, 24 + qt:25 + qt],
                             scalar1=EPS, scalar2=None, op0=ALU.max)
            vsp()
            VE.reciprocal(sm2[:, 32 + qt:33 + qt], tmv[:, 32 + qt:33 + qt])
            vsp()
            VE.tensor_tensor(out=sm2[:, 40 + qt:41 + qt], in0=sm2[:, 32 + qt:33 + qt],
                             in1=tmv[:, 16 + qt:17 + qt], op=ALU.mult)
            vsp()
            VE.tensor_scalar(out=tmv[:, 40 + qt:41 + qt], in0=sm2[:, 40 + qt:41 + qt],
                             scalar1=127.0, scalar2=None, op0=ALU.mult)
            vsp()
            VE.tensor_scalar(out=sqi[:, :], in0=xt_,
                             scalar1=tmv[:, 0 + qt:1 + qt], scalar2=tmv[:, 40 + qt:41 + qt],
                             op0=ALU.subtract, op1=ALU.mult)
            if qt >= 2:
                VE.wait_ge(c_pe.h, xq_pt[qt - 2])
            VE.tensor_copy(sqb[:, (qt % 2) * D:(qt % 2 + 1) * D], sqi[:, :])
            f_sq[qt] = c_dve.bump(VE.last)
            sxB[0] += 1

        def emit_statC():
            if sxC[0] >= sxB[0] or sxC[0] >= 8:
                return
            qt = sxC[0]
            for g in range(2):
                TE.wait_ge(c_dve.h, f_sq[qt] if g == 0 else p7_cons[0])
                if g == 0:
                    TE.wait_ge(c_dve.h, p7_cons[0] if p7_cons[0] else vtr)
                for j in range(8):
                    k = 8 * g + j
                    TE.transpose(psb7[:, 128 * j:128 * (j + 1)],
                                 sqb[:, (qt % 2) * D + 128 * k:(qt % 2) * D + 128 * (k + 1)],
                                 identb[:, :])
                p_t = c_pe.bump(TE.last)
                xq_pt[qt] = p_t
                VE.wait_ge(c_pe.h, p_t)
                VE.tensor_copy(
                    xq[:, :].rearrange("p (k t) -> p k t", k=NT)[:, 8 * g:8 * g + 8,
                                                                 128 * qt:128 * (qt + 1)],
                    psb7[:, :].rearrange("p (k t) -> p k t", k=8))
                p7_cons[0] = c_dve.bump(VE.last)
            xqd[qt] = p7_cons[0]
            sxC[0] += 1

        def slot_hook(h, p, sub):
            if p == 3 and sub == 2:
                return
            if h == 0:
                if p == 1 and sub == 2:
                    emit_scales()
                elif p >= 2 and f_scales:
                    emit_tern(3 if (p == 3 and sub == 0) else 2 if sub < 2 else 3)
                return
            if h < 3:
                if f_scales:
                    emit_tern(3 if (p == 3 and sub == 0) else 2 if sub < 2 else 3)
                return
            assert t_cur[0] == len(tern_jobs), t_cur[0]
            emit_statC()
            emit_statB()
            emit_statA()
            statA_trig()

        def emit_preamble(hh):
            nonlocal vtr, b6_cons
            for r in range(4):
                if not (hh == 0 and r == 0):
                    TE.wait_ge(c_dve.h, vtr)
                for j in range(8):
                    kt = 8 * r + j
                    TE.transpose(psb7[:, 128 * j:128 * (j + 1)],
                                 qT[:, N * hh + 128 * kt:N * hh + 128 * (kt + 1)], identb[:, :])
                p_tr = c_pe.bump(TE.last)
                VE.wait_ge(c_pe.h, p_tr)
                VE.tensor_copy(vt[:, 1024 * r:1024 * (r + 1)], psb7[:, :])
                vtr = c_dve.bump(VE.last)
            for r8 in range(8):
                VE.tensor_tensor(out=sgs[:, :], in0=vt[:, 512 * r8:512 * (r8 + 1)],
                                 in1=vt[:, 512 * r8:512 * (r8 + 1)], op=ALU.mult)
                VE.tensor_reduce(out=asmb[:, 4 * r8:4 * (r8 + 1)],
                                 in_=sgs[:, :].rearrange("p (k e) -> p k e", k=4),
                                 axis=AX.X, op=ALU.add)
            vsp()
            VE.tensor_reduce(out=asmb[:, 32:33], in_=asmb[:, 0:32], axis=AX.X, op=ALU.max)
            vsp()
            f_nq = c_dve.bump(VE.last)
            TE.wait_ge(c_dve.h, f_nq)
            if b6_cons:
                TE.wait_ge(c_dve.h, b6_cons)
            TE.transpose(bank(6, 128)[0:1, :], asmb[:, 32:33], identf[:, :])
            p_nq = c_pe.bump(TE.last)
            VE.wait_ge(c_pe.h, p_nq)
            VE.tensor_copy(asmb[0:1, 0:128], bank(6, 128)[0:1, :])
            VE.tensor_reduce(out=asmb[0:1, 128:129], in_=asmb[0:1, 0:128], axis=AX.X, op=ALU.max)
            vsp()
            VE.tensor_scalar(out=asmb[0:1, 129:130], in0=asmb[0:1, 128:129],
                             scalar1=-S_ATT, scalar2=None, op0=ALU.mult)
            vsp()
            f_c = c_dve.bump(VE.last)
            b6_cons = f_c
            TE.wait_ge(c_dve.h, f_c)
            TE.matmul(bank(6, 1), onesr[0:1, :], asmb[0:1, 129:130], start=True, stop=True)
            p_c = c_pe.bump(TE.last)
            VE.wait_ge(c_pe.h, p_c)
            VE.tensor_copy(BIASC[hh], bank(6, 1)[:, 0:1])
            vsp()
            f_ch_arr[hh] = c_dve.bump(VE.last)
            b6_cons = f_ch_arr[hh]

        for h in range(NHD):
            sac = sacc[:, 128 * (h % 2):128 * (h % 2) + 128]
            if h == 0:
                TE.wait_ge(c_dve.h, p1_dve)
                emit_preamble(0)
            SC.wait_ge(c_dve.h, f_ch_arr[h])
            if h >= 2:
                SC.wait_ge(c_dve.h, sig_red[h - 2])

            for p in range(4):
                exp_kt = [0] * KT

                def emit_y(kt, exp_kt=exp_kt, p=p):
                    par = kt % 2
                    TE.wait_ge(c_act.h, exp_kt[kt])
                    if kt == 0 and y_evac:
                        TE.wait_ge(c_dve.h, y_evac)
                    if kt == 0 and p == 0 and rb4_gate:
                        TE.wait_ge(c_dve.h, rb4_gate)
                    for j in range(2):
                        TE.matmul(ps[:, 2048 + 512 * j:2048 + 512 * (j + 1)],
                                  vt[:, 128 * kt:128 * (kt + 1)],
                                  pt[:, 1024 * par + 512 * j:1024 * par + 512 * (j + 1)],
                                  start=(kt == 0), stop=(kt == KT - 1))
                    y_last[par] = c_pe.bump(TE.last)

                for kt in range(KT):
                    par = kt % 2
                    sb = 1024 * par
                    for j in range(2):
                        cq = 2 * p + j
                        TE.matmul(ps[:, sb + 512 * j:sb + 512 * (j + 1)],
                                  qT[:, N * h + 128 * kt:N * h + 128 * (kt + 1)],
                                  qT[:, N * h + 512 * cq:N * h + 512 * (cq + 1)],
                                  start=True, stop=True)
                    sc_mk = c_pe.bump(TE.last)
                    SC.wait_ge(c_pe.h, sc_mk)
                    SC.activation(out=pt[:, 1024 * par:1024 * (par + 1)],
                                  in_=ps[:, sb:sb + 1024], func=AF.Exp,
                                  scale=S_ATT, bias=BIASC[h],
                                  accum_out=sac[:, 4 * kt + p:4 * kt + p + 1])
                    exp_kt[kt] = c_act.bump(SC.last)
                    if kt >= 1:
                        emit_y(kt - 1)
                    if kt == 10 or kt == 21:
                        slot_hook(h, p, 0 if kt == 10 else 1)
                emit_y(KT - 1)
                VE.wait_ge(c_pe.h, y_last[(KT - 1) % 2])
                VE.tensor_copy(yraw[:, 1024 * p:1024 * p + 512], bank(4))
                VE.tensor_copy(yraw[:, 1024 * p + 512:1024 * (p + 1)], bank(5))
                y_evac = c_dve.bump(VE.last)
                slot_hook(h, p, 2)
            for _ in range(4):
                SC.copy(out=sgs[:, 0:8], in_=small[:, 0:8])
            a_sig = c_act.bump(SC.last)

            if h > 0:
                dsSW.wait(VE, ms1[h - 1])
            VE.wait_ge(c_act.h, a_sig)
            VE.tensor_reduce(out=scol[:, :], in_=sac.rearrange("p (k e) -> p k e", k=KT),
                             axis=AX.X, op=ALU.add)
            vsp()
            sig_red[h] = c_dve.bump(VE.last)
            VE.reciprocal(rcol[:, :], scol[:, :])
            vsp()
            f_rc = c_dve.bump(VE.last)
            TE.wait_ge(c_dve.h, f_rc)
            if b6_cons:
                TE.wait_ge(c_dve.h, b6_cons)
            TE.transpose(bank(6, 128)[0:32, :], rcol[:, :], identf[:, :])
            p_rt = c_pe.bump(TE.last)
            VE.wait_ge(c_pe.h, p_rt)
            VE.tensor_copy(fsc[0:32, 0:128], bank(6, 128)[0:32, :])
            vsp(2)
            f_rt = c_dve.bump(VE.last)
            b6_cons = f_rt
            SY.wait_ge(c_dve.h, f_rt)
            if h > 0:
                SY.wait_ge(c_pe.h, prb_last[h - 1])
            if h == 0 and dbg.get("fst") is not None:
                SC.wait_ge(c_dve.h, f_rt)
                dsDBG.dma(dbg["fst"][:, :], fsc[0:32, 0:128])
            ms1[h] = dsSW.dma(rowq[0:1, :], fsc[0:32, 0:128])
            if h == 0 and dbg.get("swz") is not None:
                dsSW.wait(SY, ms1[h])
                dsDBG.dma(dbg["swz"][0:1, :], rowq[0:1, :])

            if h + 1 < NHD:
                emit_preamble(h + 1)

            dsSW.wait(TE, ms1[h])
            bb = (bank(6), bank(4))
            rb_cp = [0] * NCH
            for c in range(NCH):
                if c == 0:
                    TE.wait_ge(c_dve.h, b6_cons)
                elif c >= 2:
                    TE.wait_ge(c_dve.h, rb_cp[c - 2])
                TE.matmul(bb[c % 2], onesr[0:1, :], rowq[0:1, 512 * c:512 * (c + 1)],
                          start=True, stop=True)
                p_rb = c_pe.bump(TE.last)
                if c == NCH - 1:
                    prb_last[h] = p_rb
                VE.wait_ge(c_pe.h, p_rb)
                VE.tensor_copy(bcf[:, 512 * c:512 * (c + 1)], bb[c % 2])
                rb_cp[c] = c_dve.bump(VE.last)
            b6_cons = rb_cp[NCH - 2]
            rb4_gate = rb_cp[NCH - 1]
            if h == 0 and dbg.get("bcf") is not None:
                SC.wait_ge(c_dve.h, rb_cp[NCH - 1])
                dsDBG.dma(dbg["bcf"][:, :], bcf[:, :])
                dsDBG.dma(dbg["rowq"][0:1, :], rowq[0:1, :])
                dsDBG.dma(dbg["rcol"][:, :], rcol[:, :])
            if h > 0:
                dsPUB.wait(VE, pub[h - 1])
            VE.tensor_tensor(out=ynb[:, :], in0=yraw[:, :], in1=bcf[:, :], op=ALU.mult)
            f_yn = c_dve.bump(VE.last)
            f_yn_prev = f_yn
            GP.wait_ge(c_dve.h, f_yn)
            pub[h] = dsPUB.dma(
                cc2i[128 * N * h:128 * N * (h + 1)].rearrange("(p t) -> p t", p=128),
                ynb[:, :])
            if dbg.get("yn") is not None:
                dsDBG.dma(dbg["yn"][128 * h:128 * (h + 1), :], ynb[:, :])
                dsDBG.dma(dbg["sig"][128 * h:128 * (h + 1), :], scol[:, :])

        emit_tern(99)
        while sxA[0] < 8:
            statA_trig()
            emit_statA()
        while sxB[0] < 8:
            emit_statB()
        while sxC[0] < 8:
            emit_statC()

        if dbg.get("sml") is not None:
            SC.wait_ge(c_dve.h, c_dve.n)
            dsDBG.dma(dbg["sml"][:, 0:32], small[:, :])
            dsDBG.dma(dbg["sml"][:, 32:192], asmb[:, :])
        dsPUB.wait(GP, pub[NHD - 1])
        cc2d = c_gp.bump(GP.collective_compute(
            "AllGather", ALU.bypass, replica_groups=[[0, 1, 2, 3], [4, 5, 6, 7]],
            ins=[cc2i[:]], outs=[cc2o[:]]))

        if phases < 3:
            GP.wait_ge(c_gp.h, cc2d)
            drain_all()
            return nc, dbg

        f_xq = xqd[7]
        if dbg.get("xq") is not None:
            SC.wait_ge(c_dve.h, f_xq)
            dsDBG.dma(dbg["xq"][:, :], xq[:, :])

        def row_bcast(src_off, dstbuf):
            TE.wait_ge(c_dve.h, c_dve.n)
            TE.transpose(bank(6, 128)[0:8, :], tmv[:, src_off:src_off + 8], identf[:, :])
            p1 = c_pe.bump(TE.last)
            VE.wait_ge(c_pe.h, p1)
            VE.tensor_copy(asmb[0:8, 0:128], bank(6, 128)[0:8, :])
            vsp(2)
            f1 = c_dve.bump(VE.last)
            SY.wait_ge(c_dve.h, f1)
            s2 = dsSW.dma(rowq[0:1, 0:1024], asmb[0:8, 0:128])
            dsSW.wait(TE, s2)
            for cc in range(2):
                TE.matmul(bank(2 + cc), onesr[0:1, :], rowq[0:1, 512 * cc:512 * (cc + 1)],
                          start=True, stop=True)
            p2 = c_pe.bump(TE.last)
            SC.wait_ge(c_pe.h, p2)
            SC.copy(out=dstbuf[:, 0:512], in_=bank(2))
            SC.copy(out=dstbuf[:, 512:1024], in_=bank(3))
            return c_act.bump(SC.last)

        m_up = [0] * 8
        upld = [0] * 8
        upev = {}
        dsSW.wait(VE, ms1[NHD - 1])
        VE.memset(osb[:, :], 0.0)
        VE.memset(fsc[:, :], 0.0)
        c_dve.bump(VE.last)
        SY.wait_ge(c_pe.h, prb_last[NHD - 1])
        for sl in range(8):
            if sl >= 2:
                SY.wait_ge(c_pe.h, upld[sl - 2])
                SC.wait_ge(c_pe.h, upld[sl - 2])
            sb = (sl % 2) * NT * 512
            src = tup_h[:, 512 * sl:512 * (sl + 1)].rearrange("(k p) i -> p k i", p=128)
            dst = upsl[:, sb:sb + NT * 512].rearrange("p (k i) -> p k i", k=NT)
            m_up[sl] = dsUP.dma(dst[:, 0:NT // 2, :], src[:, 0:NT // 2, :])
            m_upv = dsUPV.dma(dst[:, NT // 2:NT, :], src[:, NT // 2:NT, :])
            if sl == 0:
                dsTO.drain(TE)
            dsUP.wait(TE, m_up[sl])
            dsUPV.wait(TE, m_upv)
            for ch in range(2):
                for it4 in range(4):
                    it = 4 * sl + it4
                    bkt = 2 * (it4 % 2) + ch
                    key = (it4 % 2, ch)
                    if upev.get(key):
                        TE.wait_ge(c_dve.h, upev[key])
                    if sl == 0 and it4 == 0:
                        TE.wait_ge(c_dve.h, xqd[3] if ch == 0 else xqd[7])
                    for k in range(NT):
                        TE.matmul(bank(bkt),
                                  upsl[:, sb + 512 * k + 128 * it4:sb + 512 * k + 128 * (it4 + 1)],
                                  xq[:, T * k + 512 * ch:T * k + 512 * (ch + 1)],
                                  start=(k == 0), stop=(k == NT - 1))
                    pmk = c_pe.bump(TE.last)
                    if it4 == 3 and ch == 1:
                        upld[sl] = pmk
                    SC.wait_ge(c_pe.h, pmk)
                    SC.activation(out=sgs[:, :], in_=bank(bkt), func=AF.Relu,
                                  bias=ZEROC, scale=1.0)
                    a_rl = c_act.bump(SC.last)
                    VE.wait_ge(c_act.h, a_rl)
                    VE.tensor_tensor(out=v_sb[:, T * it + 512 * ch:T * it + 512 * (ch + 1)],
                                     in0=sgs[:, :], in1=sgs[:, :], op=ALU.mult)
                    vm = osb if ch == 0 else fsc
                    VE.tensor_tensor(out=vm[:, :], in0=vm[:, :],
                                     in1=v_sb[:, T * it + 512 * ch:T * it + 512 * (ch + 1)],
                                     op=ALU.max)
                    upev[key] = c_dve.bump(VE.last)
        f_up = c_dve.n

        TE.wait_ge(c_dve.h, f_up)
        for half, vm in enumerate((osb, fsc)):
            for j in range(4):
                TE.wait_ge(c_dve.h, c_dve.n)
                TE.transpose(bank(6, 128), vm[:, 128 * j:128 * (j + 1)], identf[:, :])
                pj = c_pe.bump(TE.last)
                VE.wait_ge(c_pe.h, pj)
                qq = 4 * half + j
                VE.tensor_reduce(out=tmv[:, 48 + qq:49 + qq], in_=bank(6, 128)[:, 0:128],
                                 axis=AX.X, op=ALU.max)
                c_dve.bump(VE.last)
        vsp()
        c_dve.bump(VE.last)
        VE.tensor_scalar(out=asmb[:, 0:8], in0=tmv[:, 32:40], scalar1=SUP, scalar2=1.0 / 127.0,
                         op0=ALU.mult, op1=ALU.mult)
        vsp()
        VE.tensor_tensor(out=asmb[:, 0:8], in0=asmb[:, 0:8], in1=asmb[:, 0:8], op=ALU.mult)
        vsp()
        VE.tensor_tensor(out=asmb[:, 8:16], in0=asmb[:, 0:8], in1=tmv[:, 48:56], op=ALU.mult)
        vsp()
        VE.tensor_scalar(out=asmb[:, 8:16], in0=asmb[:, 8:16], scalar1=SDN, scalar2=1.0 / 127.0,
                         op0=ALU.mult, op1=ALU.mult)
        vsp()
        VE.tensor_copy(tmv[:, 56:64], asmb[:, 8:16])
        VE.tensor_scalar(out=asmb[:, 16:24], in0=tmv[:, 48:56], scalar1=EPS, scalar2=None,
                         op0=ALU.max)
        vsp()
        VE.reciprocal(asmb[:, 16:24], asmb[:, 16:24])
        vsp()
        VE.tensor_scalar(out=tmv[:, 48:56], in0=asmb[:, 16:24], scalar1=127.0, scalar2=None,
                         op0=ALU.mult)
        vsp()
        f_beta = c_dve.bump(VE.last)
        if dbg.get("tm") is not None:
            SC.wait_ge(c_dve.h, f_beta)
            dsDBG.dma(dbg["tm"][:, :], tmv[:, :])
        a_s2b = row_bcast(48, s2bb)

        hqe = [0] * ITL
        for it in range(ITL):
            if it == 0:
                VE.wait_ge(c_act.h, a_s2b)
            VE.tensor_tensor(out=xsi[:, :], in0=v_sb[:, T * it:T * (it + 1)],
                             in1=s2bb[:, 0:1024], op=ALU.mult)
            VE.tensor_copy(v_sb[:, T * it:T * (it + 1)], xsi[:, :])
            hqe[it] = c_dve.bump(VE.last)
        f_hq = c_dve.n
        if dbg.get("hq") is not None:
            SC.wait_ge(c_dve.h, f_hq)
            dsDBG.dma(dbg["hq"][:, :], v_sb[:, :])

        m_dn = [0] * 4
        dnld = [0] * 4
        dnev = {}
        for dc in range(4):
            gate = dnld[dc - 1] if dc >= 1 else upld[7]
            SY.wait_ge(c_pe.h, gate)
            SC.wait_ge(c_pe.h, gate)
            src = tdn_h[:, 512 * dc:512 * (dc + 1)].rearrange("(i p) e -> p i e", p=128)
            dst = dnsl[:, :].rearrange("p (i e) -> p i e", i=ITL)
            m_dn[dc] = dsDN.dma(dst[:, 0:ITL // 2, :], src[:, 0:ITL // 2, :])
            m_dnv = dsDNV.dma(dst[:, ITL // 2:ITL, :], src[:, ITL // 2:ITL, :])
            dsDN.wait(TE, m_dn[dc])
            dsDNV.wait(TE, m_dnv)
            for qt in range(8):
                bkt = 4 + (qt % 2)
                key = qt % 2
                if dnev.get(key):
                    TE.wait_ge(c_dve.h, dnev[key])
                for it in range(ITL):
                    if dc == 0 and qt == 0:
                        TE.wait_ge(c_dve.h, hqe[it])
                    TE.matmul(bank(bkt), v_sb[:, T * it + 128 * qt:T * it + 128 * (qt + 1)],
                              dnsl[:, 512 * it:512 * (it + 1)],
                              start=(it == 0), stop=(it == ITL - 1))
                pmk = c_pe.bump(TE.last)
                if qt == 7:
                    dnld[dc] = pmk
                VE.wait_ge(c_pe.h, pmk)
                VE.tensor_scalar(out=ffo[:, D * qt + 512 * dc:D * qt + 512 * (dc + 1)],
                                 in0=bank(bkt), scalar1=TM_BETA(qt), scalar2=None,
                                 op0=ALU.mult)
                dnev[key] = c_dve.bump(VE.last)
        f_dn = c_dve.n

        SY.wait_ge(c_gp.h, cc2d)
        for arm in nc.switch_core_id(engines=[SY._e], n=512):
            ts_arm = (arm.logical % 8) % 4
            src = cc2o[:].rearrange("(s h p t) -> p (s h) t", s=4, h=NHD, p=128)
            i_y = SY.dma_start(out=yallT[:, :].rearrange("p (g t) -> p g t", g=16),
                               in_=src[:, :, T * ts_arm:T * (ts_arm + 1)])
            i_y.then_inc(dsYA.sems[0], 16)
        dsYA.cnt[0] += 16
        m_yall = (0, dsYA.cnt[0])

        m_bo = dsC.dma(bob[0:1, :], bod[0:1, :])
        dsC.wait(TE, m_bo)
        TE.wait_ge(c_dve.h, f_up)
        bo_cp = [0] * 4
        for j in range(4):
            if j >= 2:
                TE.wait_ge(c_act.h, bo_cp[j - 2])
            TE.matmul(bank(j % 2), onesr[0:1, :], bob[0:1, 512 * j:512 * (j + 1)],
                      start=True, stop=True)
            pj = c_pe.bump(TE.last)
            SC.wait_ge(c_pe.h, pj)
            SC.copy(out=bob[:, 512 * j:512 * (j + 1)], in_=bank(j % 2))
            bo_cp[j] = c_act.bump(SC.last)
        a_bo = bo_cp[3]

        m_wo = [0] * 4
        wold = [0] * 4
        woev = {}
        st_marks = []
        dve_at_tile = []
        m_xt = {}
        GP.wait_ge(c_dve.h, f_dn)
        SC.wait_ge(c_dve.h, f_dn)
        for dc in range(4):
            wb = wob if dc % 2 == 0 else wob2
            if dc >= 2:
                GP.wait_ge(c_pe.h, wold[dc - 2])
                SC.wait_ge(c_pe.h, wold[dc - 2])
            m_wo[dc] = dsWO.dma(
                wb[:, :].rearrange("p (k e) -> p k e", k=NT),
                wod[:, 512 * dc:512 * (dc + 1)].rearrange("(k p) e -> p k e", p=128))
            dsWO.wait(TE, m_wo[dc])
            if dc == 0:
                dsYA.wait(TE, m_yall)
                TE.wait_ge(c_act.h, a_bo)
            for qt in range(8):
                ii = 8 * dc + qt
                slot = ii % 2
                if ii >= 2:
                    SY.wait_ge(c_dve.h, dve_at_tile[ii - 2])
                m_xt[ii] = dsXT.dma(
                    xss[:, 512 * slot:512 * (slot + 1)],
                    x_sl[128 * qt:128 * (qt + 1), 512 * dc:512 * (dc + 1)])
                bkt = 2 + (qt % 2)
                key = qt % 2
                if woev.get(key):
                    TE.wait_ge(c_dve.h, woev[key])
                for k in range(NT):
                    TE.matmul(bank(bkt), yallT[:, T * k + 128 * qt:T * k + 128 * (qt + 1)],
                              wb[:, 512 * k:512 * (k + 1)],
                              start=(k == 0), stop=(k == NT - 1))
                pmk = c_pe.bump(TE.last)
                if qt == 7:
                    wold[dc] = pmk
                ob = osb if ii % 2 == 0 else fsc
                VE.wait_ge(c_pe.h, pmk)
                dsXT.wait(VE, m_xt[ii])
                if len(st_marks) >= 2:
                    dsST.wait(VE, st_marks[-2])
                VE.scalar_tensor_tensor(out=ob[:, :], in0=bank(bkt), scalar=1.0,
                                        in1=bob[:, 512 * dc:512 * (dc + 1)],
                                        op0=ALU.mult, op1=ALU.add)
                VE.tensor_tensor(out=ob[:, :], in0=ob[:, :],
                                 in1=ffo[:, D * qt + 512 * dc:D * qt + 512 * (dc + 1)],
                                 op=ALU.add)
                VE.tensor_tensor(out=ob[:, :], in0=ob[:, :],
                                 in1=xss[:, 512 * slot:512 * (slot + 1)], op=ALU.add)
                woev[key] = c_dve.bump(VE.last)
                dve_at_tile.append(woev[key])
                SC.wait_ge(c_dve.h, woev[key])
                st_marks.append(dsST.dma(
                    outd[128 * qt:128 * (qt + 1), 512 * dc:512 * (dc + 1)], ob[:, :]))

        drain_all()

    return nc, dbg


_CACHE = {}


def _build(phases=3, debug=0):
    key = (phases, debug)
    if key not in _CACHE:
        _CACHE[key] = build_nc(phases=phases, debug=debug)
    return _CACHE[key]


def make_in_maps(x, Wq, Wo, bo, w_up, w_dn):
    wupT = np.ascontiguousarray(w_up.T)
    wdnT = np.ascontiguousarray(w_dn.T)
    idb = np.eye(128, dtype=BF)
    idf = np.eye(128, dtype=np.float32)
    bo2 = np.ascontiguousarray(bo.reshape(1, D))
    maps = []
    for c in range(8):
        b, hg, ts = c // 4, c % 4, c % 4
        xTb = np.ascontiguousarray(x[b].T)
        maps.append({
            "xT": xTb,
            "xslT": np.ascontiguousarray(xTb[:, T * ts:T * (ts + 1)]),
            "x_sl": np.ascontiguousarray(x[b, T * ts:T * (ts + 1), :]),
            "wqd": np.ascontiguousarray(Wq[:, 512 * hg:512 * (hg + 1)]),
            "wod": Wo,
            "wupT": wupT,
            "wdnT": wdnT,
            "bod": bo2,
            "wshu": np.ascontiguousarray(wupT[256 * c:256 * (c + 1), :]),
            "wshd": np.ascontiguousarray(wdnT[512 * c:512 * (c + 1), :]),
            "idbd": idb,
            "idfd": idf,
        })
    return maps


def kernel(L0_attn_x, L0_attn_Wq, L0_attn_Wo, L0_attn_bo, L0_ffn_w_up, L0_ffn_w_down):
    x = np.asarray(L0_attn_x, dtype=np.float32)
    maps = make_in_maps(x, np.asarray(L0_attn_Wq, np.float32),
                        np.asarray(L0_attn_Wo, np.float32),
                        np.asarray(L0_attn_bo, np.float32),
                        np.asarray(L0_ffn_w_up, np.float32),
                        np.asarray(L0_ffn_w_down, np.float32))
    nc, _ = _build(3, 0)
    res = run_bass_kernel_spmd(nc, maps, core_ids=list(range(8)))
    o = np.empty((B, N, D), dtype=np.float32)
    for c in range(8):
        b, ts = c // 4, c % 4
        o[b, T * ts:T * (ts + 1), :] = res.results[c]["outd"]
    return o


# version 14
# speedup vs baseline: 1.1905x; 1.0888x over previous
import numpy as np
import ml_dtypes
import concourse.bass as bass
import concourse.mybir as mybir
from concourse.bass_utils import run_bass_kernel_spmd

F32 = mybir.dt.float32
BF16 = mybir.dt.bfloat16
I32 = mybir.dt.int32
AF = mybir.ActivationFunctionType
ALU = mybir.AluOpType
AX = mybir.AxisListType

B, N, D = 2, 4096, 2048
T = 1024
I = 4096
NT = D // 128
NCH = N // 512
KT = N // 128
ITL = I // 128
NHD = 4
S_ATT = float(128.0 ** -0.5)
EPS = 1e-5
BF = ml_dtypes.bfloat16
GS = [3] * 10 + [2]


class Ctr:
    def __init__(self, nc, name):
        self.h = nc.alloc_semaphore(name=name)
        self.n = 0
        self._seen = []

    def bump(self, instr, k=1):
        assert id(instr) not in {id(o) for o in self._seen}, "double then_inc"
        self._seen.append(instr)
        instr.then_inc(self.h, k)
        self.n += k
        return self.n


class DStream:

    def __init__(self, nc, eng, name, nbuf=2):
        self.eng = eng
        self.sems = [nc.alloc_semaphore(name=f"{name}{i}") for i in range(nbuf)]
        self.cnt = [0] * nbuf
        self.i = 0

    def dma(self, out, in_):
        j = self.i % len(self.sems)
        instr = self.eng.dma_start(out=out, in_=in_)
        instr.then_inc(self.sems[j], 16)
        self.cnt[j] += 16
        self.i += 1
        return (j, self.cnt[j])

    def wait(self, weng, tok):
        if isinstance(tok, list):
            for t in tok:
                self.wait(weng, t)
            return
        j, v = tok
        weng.wait_ge(self.sems[j], v)

    def drain(self, weng):
        for j in range(len(self.sems)):
            if self.cnt[j]:
                weng.wait_ge(self.sems[j], self.cnt[j])


class Eng:

    _NOTRACK = {"wait_ge", "sem_clear", "last"}

    def __init__(self, eng):
        object.__setattr__(self, "_e", eng)
        object.__setattr__(self, "last", None)

    def __getattr__(self, n):
        f = getattr(self._e, n)
        if not callable(f) or n in self._NOTRACK or n.startswith("_"):
            return f
        def wrap(*a, **k):
            r = f(*a, **k)
            if hasattr(r, "then_inc"):
                object.__setattr__(self, "last", r)
            return r
        return wrap


def build_nc(phases=3, debug=0, stop=None):
    nc = bass.Bass("TRN2", target_bir_lowering=False, detect_race_conditions=False)

    xT = nc.dram_tensor("xT", [D, N], F32, kind="ExternalInput").ap()
    xslT = nc.dram_tensor("xslT", [D, T], F32, kind="ExternalInput").ap()
    x_sl = nc.dram_tensor("x_sl", [T, D], F32, kind="ExternalInput").ap()
    wqd = nc.dram_tensor("wqd", [D, 512], F32, kind="ExternalInput").ap()
    wod = nc.dram_tensor("wod", [D, D], F32, kind="ExternalInput").ap()
    wupT = nc.dram_tensor("wupT", [D, I], F32, kind="ExternalInput").ap()
    wdnT = nc.dram_tensor("wdnT", [I, D], F32, kind="ExternalInput").ap()
    bod = nc.dram_tensor("bod", [1, D], F32, kind="ExternalInput").ap()
    wshu = nc.dram_tensor("wshu", [256, I], F32, kind="ExternalInput").ap()
    wshd = nc.dram_tensor("wshd", [512, D], F32, kind="ExternalInput").ap()
    idbd = nc.dram_tensor("idbd", [128, 128], BF16, kind="ExternalInput").ap()
    idfd = nc.dram_tensor("idfd", [128, 128], F32, kind="ExternalInput").ap()
    outd = nc.dram_tensor("outd", [T, D], F32, kind="ExternalOutput").ap()

    tup_h = nc.dram_tensor("tup_h", [8, 128, NT * 512], BF16, kind="Internal").ap()
    tdn_h = nc.dram_tensor("tdn_h", [4, 128, ITL * 512], BF16, kind="Internal").ap()
    cc1i = nc.dram_tensor("cc1i", [1, 2], F32, kind="Internal").ap()
    cc1o = nc.dram_tensor("cc1o", [8, 2], F32, kind="Internal").ap()
    cc2i = nc.dram_tensor("cc2i", [NHD * 128 * N], BF16, kind="Internal").ap()
    cc2o = nc.dram_tensor("cc2o", [4 * NHD * 128 * N], BF16, kind="Internal").ap()
    swzd = nc.dram_tensor("swzd", [4096], F32, kind="Internal").ap()

    dbg = {}
    if debug >= 1:
        dbg["qT"] = nc.dram_tensor("dbg_qT", [128, NHD * N], BF16, kind="ExternalOutput").ap()
        if phases >= 2:
            dbg["yn"] = nc.dram_tensor("dbg_yn", [NHD * 128, N], BF16, kind="ExternalOutput").ap()
            dbg["sig"] = nc.dram_tensor("dbg_sig", [NHD * 128, KT], F32, kind="ExternalOutput").ap()
            dbg["sml"] = nc.dram_tensor("dbg_sml", [128, 32 + 160], F32, kind="ExternalOutput").ap()
            dbg["bcf"] = nc.dram_tensor("dbg_bcf", [128, N], F32, kind="ExternalOutput").ap()
            dbg["rowq"] = nc.dram_tensor("dbg_rowq", [1, N], F32, kind="ExternalOutput").ap()
            dbg["rcol"] = nc.dram_tensor("dbg_rcol", [128, 32], F32, kind="ExternalOutput").ap()
            dbg["fst"] = nc.dram_tensor("dbg_fst", [32, 128], F32, kind="ExternalOutput").ap()
            dbg["swz"] = nc.dram_tensor("dbg_swz", [1, 4096], F32, kind="ExternalOutput").ap()
        if phases >= 3:
            dbg["xq"] = nc.dram_tensor("dbg_xq", [128, NT * T], BF16, kind="ExternalOutput").ap()
            dbg["hq"] = nc.dram_tensor("dbg_hq", [128, ITL * T], BF16, kind="ExternalOutput").ap()
            dbg["tm"] = nc.dram_tensor("dbg_tm", [128, 64], F32, kind="ExternalOutput").ap()

    SBASE = 16512

    def at(name, shape, dtype, off):
        return nc.alloc_sbuf_tensor_at(name, shape, dtype, offset=SBASE + off)

    qT = at("qT", [128, NHD * N], BF16, 0x00000)
    upsl = at("upsl", [128, 2 * NT * 512], BF16, 0x00000)
    dnsl = at("dnsl", [128, ITL * 512], BF16, 0x00000)
    xb = at("xb", [128, 2 * NT * 512], BF16, 0x08000)
    xq = at("xq", [128, NT * T], BF16, 0x08000)
    ffo = at("ffo", [128, 8 * D], BF16, 0x08000)
    wqb = at("wqb", [128, NT * 512], BF16, 0x10000)
    xsf = at("xsf", [128, 2 * T], F32, 0x10000)
    xsi = at("xsi", [128, T], I32, 0x12000)
    wob = at("wob", [128, NT * 512], BF16, 0x10000)
    wob2 = at("wob2", [128, NT * 512], BF16, 0x14000)
    rowq = at("rowq", [1, N], F32, 0x10000)
    wtf = at("wtf", [128, 2 * 4096], F32, 0x14000)
    ti32 = at("ti32", [128, 4096], I32, 0x1C000)
    tbo = at("tbo", [128, 4096], BF16, 0x20000)
    tbo2 = at("tbo2", [128, 4096], BF16, 0x22000)
    x2w = at("x2w", [128, 8192], BF16, 0x20000)
    v_sb = at("v_sb", [128, ITL * T], BF16, 0x14000)
    bcf = at("bcf", [128, N], F32, 0x24000)
    yraw = at("yraw", [128, N], F32, 0x28000)
    yallT = at("yallT", [128, 16 * T], BF16, 0x24000)
    xss = at("xss", [128, 2 * 512], F32, 0x2C000)
    vt = at("vt", [128, N], BF16, 0x2C000)
    xst = at("xst", [128, 2 * D], F32, 0x14000)
    sqi = at("sqi", [128, D], I32, 0x1C000)
    sqb = at("sqb", [128, 2 * D], BF16, 0x20000)
    s2bb = at("s2bb", [128, 2 * 512], F32, 0x2E000)
    pt = at("pt", [128, 2 * 1536], BF16, 0x2E000)
    ynb = at("ynb", [128, N], BF16, 0x2F800)
    bob = at("bob", [128, D], F32, 0x2F800)
    x2s = at("x2s", [128, 512], F32, 0x1C000)
    x2t = at("x2t", [128, 512], F32, 0x1C800)
    rbc = at("rbc", [128, 512], F32, 0x2E000)
    mbc = at("mbc", [128, 512], F32, 0x2E800)
    rowc = at("rowc", [1, 512], F32, 0x2F000)
    wbr = at("wbr", [1, 512], F32, 0x2F800)
    osb = at("osb", [128, 512], F32, 0x31800)
    fsc = at("fsc", [128, 512], F32, 0x32000)
    sgs = at("sgs", [128, 512], BF16, 0x32800)
    scol = at("scol", [128, 32], F32, 0x32C00)
    rcol = at("rcol", [128, 32], F32, 0x32C80)
    asmb = at("asmb", [128, 160], F32, 0x32D00)
    sm2 = at("sm2", [128, 64], F32, 0x32F80)
    tmv = at("tmv", [128, 64], F32, 0x33080)
    small = at("small", [128, 32], F32, 0x33180)
    identb = at("identb", [128, 128], BF16, 0x33200)
    identf = at("identf", [128, 128], F32, 0x33300)
    onesb = at("onesb", [128, 1], BF16, 0x33500)
    onesr = at("onesr", [1, 128], F32, 0x33520)
    sacc = at("sacc", [128, 256], F32, 0x33720)
    assert SBASE + 0x33720 + 1024 < 229344

    ps = nc.alloc_psum_tensor("ps", [128, 3584], F32)
    psb7 = nc.alloc_psum_tensor("psb7", [128, 1024], BF16)

    def bank(i, cols=512):
        return ps[:, i * 512:i * 512 + cols]

    c_pe = Ctr(nc, "c_pe")
    c_act = Ctr(nc, "c_act")
    c_dve = Ctr(nc, "c_dve")
    c_gp = Ctr(nc, "c_gp")

    TE = Eng(nc.tensor)
    SC = Eng(nc.scalar)
    VE = Eng(nc.vector)
    GP = Eng(nc.gpsimd)
    SY = Eng(nc.sync)

    dsC = DStream(nc, nc.sync, "sC", 1)
    dsSH = DStream(nc, nc.sync, "sSH", 2)
    dsWQ = DStream(nc, nc.gpsimd, "sWQ", 1)
    dsXB = DStream(nc, nc.gpsimd, "sXB", 2)
    dsCC = DStream(nc, nc.gpsimd, "sCC", 1)
    dsSW = DStream(nc, nc.sync, "sSW", 1)
    dsPUB = DStream(nc, nc.gpsimd, "sPUB", 1)
    dsTU = DStream(nc, nc.gpsimd, "sTU", 2)
    dsTD = DStream(nc, nc.gpsimd, "sTD", 2)
    dsTO = DStream(nc, nc.gpsimd, "sTO", 2)
    dsXS = DStream(nc, nc.sync, "sXS", 2)
    dsXF = DStream(nc, nc.sync, "sXF", 2)
    dsUP = DStream(nc, nc.sync, "sUP", 2)
    dsUPV = DStream(nc, nc.scalar, "sUPV", 2)
    dsDN = DStream(nc, nc.sync, "sDN", 1)
    dsDNV = DStream(nc, nc.scalar, "sDNV", 1)
    dsWOV = DStream(nc, nc.scalar, "sWOV", 2)
    dsYA = DStream(nc, nc.sync, "sYA", 1)
    dsWO = DStream(nc, nc.gpsimd, "sWO", 2)
    dsXT = DStream(nc, nc.sync, "sXT", 2)
    dsST = DStream(nc, nc.scalar, "sST", 2)
    dsDBG = DStream(nc, nc.scalar, "sDBG", 1)
    ALL_DS = [dsC, dsSH, dsWQ, dsXB, dsCC, dsSW, dsPUB, dsTU, dsTD, dsTO,
              dsXS, dsXF, dsUP, dsUPV, dsDN, dsDNV, dsWOV, dsYA, dsWO, dsXT, dsST, dsDBG]

    ONESF = small[:, 0:1]
    EPSC = small[:, 1:2]
    ZEROC = small[:, 2:3]
    ACCFL = small[:, 3:4]
    BIASC = [small[:, 4 + i:5 + i] for i in range(NHD)]
    SUP = small[:, 8:9]
    SDN = small[:, 9:10]
    INVSUP = small[:, 10:11]
    INVSDN = small[:, 11:12]
    WCOL = [sm2[:, 16 + t:17 + t] for t in range(4)]
    TM_BETA = lambda q: tmv[:, 56 + q:57 + q]

    def vsp(n=4):
        for _ in range(n):
            VE.tensor_copy(sm2[:, 63:64], sm2[:, 62:63])

    def drain_all():
        for ds in ALL_DS:
            ds.drain(GP)
        GP.wait_ge(c_dve.h, c_dve.n)
        GP.wait_ge(c_pe.h, c_pe.n)
        GP.wait_ge(c_act.h, c_act.n)

    def fin(tag):
        if stop != tag:
            return False
        drain_all()
        return True

    with nc.Block() as block:
        dsC.dma(identb[:, :], idbd[:, :])
        t_id = dsC.dma(identf[:, :], idfd[:, :])
        t_shu = [dsSH.dma(wtf[:, 4096 * i:4096 * (i + 1)],
                          wshu[128 * i:128 * (i + 1), :]) for i in range(2)]
        VE.memset(ONESF, 1.0)
        VE.memset(EPSC, EPS)
        VE.memset(ZEROC, 0.0)
        VE.memset(onesb[:, :], 1.0)
        VE.memset(onesr[0:1, :], 1.0)
        f_const = c_dve.bump(VE.memset(scol[:, :], 0.0))
        SC.wait_ge(c_dve.h, f_const)
        SC.activation(out=sgs[:, 0:8], in_=small[:, 0:8], func=AF.Exp, scale=0.0,
                      bias=ZEROC, accum_out=ACCFL)
        dsSH.wait(VE, t_shu[0])
        VE.tensor_reduce(out=asmb[:, 150:151], in_=wtf[:, 0:4096], axis=AX.X,
                         op=ALU.add, apply_absolute_value=True)
        dsSH.wait(VE, t_shu[1])
        VE.tensor_reduce(out=asmb[:, 151:152], in_=wtf[:, 4096:8192], axis=AX.X,
                         op=ALU.add, apply_absolute_value=True)
        f_shu = c_dve.bump(VE.last)
        t_shd = [None] * 4
        sh_red = [0] * 4
        for i in range(4):
            if i == 0:
                SY.wait_ge(c_dve.h, f_shu)
            elif i >= 2:
                SY.wait_ge(c_dve.h, sh_red[i - 2])
            t_shd[i] = dsSH.dma(wtf[:, 2048 * i:2048 * (i + 1)],
                                wshd[128 * i:128 * (i + 1), :])
            dsSH.wait(VE, t_shd[i])
            VE.tensor_reduce(out=asmb[:, 152 + i:153 + i], in_=wtf[:, 2048 * i:2048 * (i + 1)],
                             axis=AX.X, op=ALU.add, apply_absolute_value=True)
            sh_red[i] = c_dve.bump(VE.last)
        VE.tensor_reduce(out=sm2[:, 0:1], in_=asmb[:, 150:152], axis=AX.X, op=ALU.add)
        VE.tensor_reduce(out=sm2[:, 1:2], in_=asmb[:, 152:156], axis=AX.X, op=ALU.add)
        vsp()
        f_shd = c_dve.bump(VE.last)
        if fin("ph0a"):
            return nc, dbg
        TE.wait_ge(c_dve.h, f_shd)
        dsC.wait(TE, t_id)
        TE.transpose(bank(6, 128)[0:2, :], sm2[:, 0:2], identf[:, :])
        p_t0 = c_pe.bump(TE.last)
        VE.wait_ge(c_pe.h, p_t0)
        VE.tensor_copy(asmb[0:2, 0:128], bank(6, 128)[0:2, :])
        VE.tensor_reduce(out=asmb[0:2, 128:129], in_=asmb[0:2, 0:128], axis=AX.X, op=ALU.add)
        vsp()
        f_par = c_dve.bump(VE.last)
        if fin("ph0b"):
            return nc, dbg

        t_wq = dsWQ.dma(wqb[:, :].rearrange("p (k c) -> p k c", k=NT),
                        wqd.rearrange("(k p) c -> p k c", p=128))
        dsWQ.wait(TE, t_wq)
        TE.wait_ge(c_dve.h, f_par)
        for t in range(4):
            for k in range(NT):
                TE.matmul(bank(2, 1), wqb[:, 512 * k + 128 * t:512 * k + 128 * (t + 1)],
                          onesb[:, :], start=(k == 0), stop=(k == NT - 1))
            pw = c_pe.bump(TE.last)
            VE.wait_ge(c_pe.h, pw)
            VE.tensor_copy(WCOL[t], bank(2, 1)[:, 0:1])
            vsp(2)
            wc_ve = c_dve.bump(VE.last)
            TE.wait_ge(c_dve.h, wc_ve)
        f_wcol = c_dve.n
        if fin("wbar"):
            return nc, dbg

        mx = [0] * NCH
        st_ev = [0] * NCH
        rows_rdy = [0] * NCH
        bc_mm = [0] * NCH
        qld = [0] * NCH
        x2mm = [0] * NCH
        q_ev = [0] * (4 * NCH)
        f_mr_list = [0] * NCH
        for c in range(NCH):
            half = (c % 2) * NT * 512

            def xbt(k, half=half):
                return xb[:, half + 512 * k:half + 512 * (k + 1)]

            if c >= 2:
                GP.wait_ge(c_pe.h, qld[c - 2])
            mx[c] = dsXB.dma(
                xb[:, half:half + NT * 512].rearrange("p (k t) -> p k t", k=NT),
                xT[:, 512 * c:512 * (c + 1)].rearrange("(k p) t -> p k t", p=128))

            dsXB.wait(TE, mx[c])
            if c >= 1:
                TE.wait_ge(c_act.h, st_ev[c - 1])
            for k in range(NT):
                TE.matmul(bank(0)[0:1, :], onesb[:, :], xbt(k), start=(k == 0), stop=(k == NT - 1))
            c_pe.bump(TE.last)
            dsXB.wait(VE, mx[c])
            if c >= 1:
                VE.wait_ge(c_pe.h, bc_mm[c - 1])
            if c >= 1:
                VE.wait_ge(c_pe.h, x2mm[c - 1])
            for j in range(4):
                VE.tensor_tensor(out=x2w[:, 2048 * j:2048 * (j + 1)],
                                 in0=xb[:, half + 2048 * j:half + 2048 * (j + 1)],
                                 in1=xb[:, half + 2048 * j:half + 2048 * (j + 1)],
                                 op=ALU.mult)
            f_x2 = c_dve.bump(VE.last)
            TE.wait_ge(c_dve.h, f_x2)
            for k in range(NT):
                TE.matmul(bank(1)[0:1, :], onesb[:, :], x2w[:, 512 * k:512 * (k + 1)],
                          start=(k == 0), stop=(k == NT - 1))
            x2mm[c] = c_pe.bump(TE.last)
            p_sxx = c_pe.n
            SC.wait_ge(c_pe.h, p_sxx)
            if c >= 1:
                SC.wait_ge(c_dve.h, q_ev[4 * (c - 1) + 3])
            SC.copy(out=rowc[0:1, :], in_=bank(0)[0:1, :])
            SC.copy(out=x2s[0:1, 0:512], in_=bank(1)[0:1, :])
            st_ev[c] = c_act.bump(SC.last)
            VE.wait_ge(c_act.h, st_ev[c])
            VE.tensor_scalar(out=rowc[0:1, :], in0=rowc[0:1, :], scalar1=1.0 / D,
                             scalar2=None, op0=ALU.mult)
            VE.tensor_scalar(out=x2s[0:1, 0:512], in0=x2s[0:1, 0:512], scalar1=1.0 / D,
                             scalar2=None, op0=ALU.mult)
            VE.tensor_tensor(out=x2t[0:1, 0:512], in0=rowc[0:1, :], in1=rowc[0:1, :], op=ALU.mult)
            VE.tensor_tensor(out=x2s[0:1, 0:512], in0=x2s[0:1, 0:512], in1=x2t[0:1, 0:512],
                             op=ALU.subtract)
            f_var = c_dve.bump(VE.last)
            SC.wait_ge(c_dve.h, f_var)
            SC.activation(out=x2s[0:1, 0:512], in_=x2s[0:1, 0:512], func=AF.Ln,
                          bias=EPSC[0:1, :], scale=1.0)
            SC.activation(out=x2s[0:1, 0:512], in_=x2s[0:1, 0:512], func=AF.Exp,
                          bias=ZEROC[0:1, :], scale=-0.5)
            a_r = c_act.bump(SC.last)
            VE.wait_ge(c_act.h, a_r)
            VE.tensor_tensor(out=x2t[0:1, 0:512], in0=rowc[0:1, :], in1=x2s[0:1, 0:512],
                             op=ALU.mult)
            VE.tensor_scalar(out=x2t[0:1, 0:512], in0=x2t[0:1, 0:512], scalar1=-1.0,
                             scalar2=None, op0=ALU.mult)
            f_mr_list[c] = c_dve.bump(VE.last)
            TE.wait_ge(c_dve.h, f_mr_list[c])
            TE.matmul(bank(5), onesr[0:1, :], x2s[0:1, 0:512], start=True, stop=True)
            TE.matmul(bank(6), onesr[0:1, :], x2t[0:1, 0:512], start=True, stop=True)
            bc_mm[c] = c_pe.bump(TE.last)
            SC.wait_ge(c_pe.h, bc_mm[c])
            SC.copy(out=rbc[:, :], in_=bank(5))
            SC.copy(out=mbc[:, :], in_=bank(6))
            rows_rdy[c] = c_act.bump(SC.last)

            for t in range(4):
                ii = 4 * c + t
                bq = bank(3 + (ii % 2))
                if ii >= 2:
                    TE.wait_ge(c_dve.h, q_ev[ii - 2])
                for k in range(NT):
                    TE.matmul(bq, wqb[:, 512 * k + 128 * t:512 * k + 128 * (t + 1)],
                              xbt(k), start=(k == 0), stop=(k == NT - 1))
                qp = c_pe.bump(TE.last)
                if t == 3:
                    qld[c] = qp
                VE.wait_ge(c_pe.h, qp)
                if t == 0:
                    VE.wait_ge(c_act.h, rows_rdy[c])
                VE.tensor_tensor(out=fsc[:, :], in0=bq, in1=rbc[:, :], op=ALU.mult)
                VE.scalar_tensor_tensor(out=qT[:, N * t + 512 * c:N * t + 512 * (c + 1)],
                                        in0=mbc[:, :], scalar=WCOL[t], in1=fsc[:, :],
                                        op0=ALU.mult, op1=ALU.add)
                q_ev[ii] = c_dve.bump(VE.last)
            if stop == "c" + str(c):
                break

        if stop is not None and stop.startswith("c") and fin(stop):
            return nc, dbg
        p1_dve = c_dve.n

        GP.wait_ge(c_dve.h, f_par)
        gm1 = dsCC.dma(cc1i[0:1, :].rearrange("a (b c) -> (a b) c", b=2),
                       asmb[0:2, 128:129])
        dsCC.wait(GP, gm1)
        cc1d = c_gp.bump(GP.collective_compute(
            "AllGather", ALU.bypass, replica_groups=[[0, 1, 2, 3, 4, 5, 6, 7]],
            ins=[cc1i[:, :]], outs=[cc1o[:, :]]))
        GP.wait_ge(c_gp.h, cc1d)
        gm2 = dsCC.dma(asmb[0:1, 132:148].rearrange("p (x y) -> p x y", x=8),
                       cc1o[:, :].rearrange("(u a) b -> u a b", u=1))

        if dbg.get("qT") is not None:
            SC.wait_ge(c_dve.h, p1_dve)
            dsDBG.dma(dbg["qT"][:, :], qT[:, :])

        if phases < 2:
            drain_all()
            return nc, dbg

        tern_up = [0] * NT
        tern_dn = [0] * ITL
        m_wup = [0] * NT
        m_wdn = [0] * ITL
        pub = [0] * NHD
        ms1 = [0] * NHD
        prb_last = [0] * NHD
        sig_red = [0] * NHD
        vtr = 0
        y_evac = 0
        b6_cons = 0
        f_yn_prev = 0
        y_last = {0: 0, 1: 0}
        f_scales = 0
        rb4_gate = 0
        f_ch_arr = [0] * NHD
        to_tok = [None] * (NT + ITL)
        tern_jobs = [("up", k) for k in range(NT)] + [("dn", k) for k in range(ITL)]
        t_cur = [0]

        def emit_scales():
            nonlocal f_scales, b6_cons
            dsCC.wait(VE, gm2)
            VE.tensor_reduce(out=asmb[0:1, 148:149], in_=asmb[0:1, 132:148:2],
                             axis=AX.X, op=ALU.add)
            VE.tensor_reduce(out=asmb[0:1, 149:150], in_=asmb[0:1, 133:148:2],
                             axis=AX.X, op=ALU.add)
            vsp()
            VE.tensor_scalar(out=asmb[0:1, 148:150], in0=asmb[0:1, 148:150],
                             scalar1=1.0 / (I * D), scalar2=EPS, op0=ALU.mult, op1=ALU.max)
            vsp()
            f_s = c_dve.bump(VE.last)
            TE.wait_ge(c_dve.h, f_s)
            TE.wait_ge(c_dve.h, b6_cons)
            TE.matmul(bank(6, 2), onesr[0:1, :], asmb[0:1, 148:150], start=True, stop=True)
            p_rep = c_pe.bump(TE.last)
            VE.wait_ge(c_pe.h, p_rep)
            VE.tensor_copy(SUP, bank(6, 2)[:, 0:1])
            VE.tensor_copy(SDN, bank(6, 2)[:, 1:2])
            vsp()
            VE.reciprocal(INVSUP, SUP)
            VE.reciprocal(INVSDN, SDN)
            vsp()
            f_scales = c_dve.bump(VE.last)
            b6_cons = f_scales

        tt_cur = [0]

        def tern_trigger(idx):
            kind, k = tern_jobs[idx]
            if kind == "up":
                if k >= 2:
                    GP.wait_ge(c_dve.h, tern_up[k - 2])
                m_wup[k] = dsTU.dma(wtf[:, (k % 2) * 4096:(k % 2 + 1) * 4096],
                                    wupT[128 * k:128 * (k + 1), :])
            else:
                if k >= 2:
                    GP.wait_ge(c_dve.h, tern_dn[k - 2])
                elif k == 0:
                    GP.wait_ge(c_dve.h, tern_up[NT - 2])
                m_wdn[k] = dsTD.dma(wtf[:, (k % 2) * 2048:(k % 2) * 2048 + 2048],
                                    wdnT[128 * k:128 * (k + 1), :])

        def emit_tern(units):
            while units > 0 and t_cur[0] < len(tern_jobs):
                while tt_cur[0] <= min(len(tern_jobs) - 1, t_cur[0] + 1):
                    tern_trigger(tt_cur[0])
                    tt_cur[0] += 1
                kind, k = tern_jobs[t_cur[0]]
                if kind == "up":
                    if k == 0:
                        VE.wait_ge(c_dve.h, f_scales)
                    dsTU.wait(VE, m_wup[k])
                    tb = tbo if k % 2 == 0 else tbo2
                    if k >= 2:
                        dsTO.wait(VE, to_tok[k - 2])
                    VE.tensor_scalar(out=ti32[:, :], in0=wtf[:, (k % 2) * 4096:(k % 2 + 1) * 4096],
                                     scalar1=INVSUP, scalar2=None, op0=ALU.mult)
                    VE.tensor_scalar(out=tb[:, :], in0=ti32[:, :], scalar1=1.0, scalar2=-1.0,
                                     op0=ALU.min, op1=ALU.max)
                    tern_up[k] = c_dve.bump(VE.last)
                    GP.wait_ge(c_dve.h, tern_up[k])
                    toks = [dsTO.dma(tup_h[sl, :, 512 * k:512 * (k + 1)],
                                     tb[:, 512 * sl:512 * (sl + 1)]) for sl in range(8)]
                    to_tok[k] = toks[-2:]
                    units -= 2
                else:
                    dsTD.wait(VE, m_wdn[k])
                    tb = tbo if k % 2 == 0 else tbo2
                    dsTO.wait(VE, to_tok[NT + k - 2] if k >= 2 else to_tok[NT - 2 + k])
                    VE.tensor_scalar(out=ti32[:, 0:2048],
                                     in0=wtf[:, (k % 2) * 2048:(k % 2) * 2048 + 2048],
                                     scalar1=INVSDN, scalar2=None, op0=ALU.mult)
                    VE.tensor_scalar(out=tb[:, 0:2048], in0=ti32[:, 0:2048], scalar1=1.0,
                                     scalar2=-1.0, op0=ALU.min, op1=ALU.max)
                    tern_dn[k] = c_dve.bump(VE.last)
                    GP.wait_ge(c_dve.h, tern_dn[k])
                    toks = [dsTO.dma(tdn_h[dcc, :, 512 * k:512 * (k + 1)],
                                     tb[:, 512 * dcc:512 * (dcc + 1)]) for dcc in range(4)]
                    to_tok[NT + k] = toks[-2:]
                    units -= 1
                t_cur[0] += 1

        sxA = [0]
        sxB = [0]
        sxC = [0]
        xqd = [0] * 8
        f_st = [0] * 8
        f_sq = [0] * 8
        a_r = [0] * 8
        m_xs = [0] * 8
        xq_pt = [0] * 8
        p7_cons = [0]

        sxT = [0]

        def statA_trig():
            if sxT[0] >= 8:
                return
            qt = sxT[0]
            if qt == 0:
                SY.wait_ge(c_dve.h, tern_dn[ITL - 1])
            elif qt >= 2:
                SY.wait_ge(c_dve.h, f_sq[qt - 2])
            m_xs[qt] = dsXS.dma(xst[:, (qt % 2) * D:(qt % 2 + 1) * D],
                                x_sl[128 * qt:128 * (qt + 1), :])
            sxT[0] += 1

        def emit_statA():
            if sxA[0] >= 8 or sxA[0] >= sxT[0]:
                return
            qt = sxA[0]
            dsXS.wait(VE, m_xs[qt])
            xt_ = xst[:, (qt % 2) * D:(qt % 2 + 1) * D]
            for j in range(4):
                VE.bn_stats(out=asmb[:, 6 * j:6 * (j + 1)],
                            in_=xt_[:, 512 * j:512 * (j + 1)])
            vsp()
            VE.bn_aggr(out=asmb[:, 24:26], in_=asmb[:, 0:24])
            vsp()
            VE.tensor_copy(tmv[:, 0 + qt:1 + qt], asmb[:, 24:25])
            VE.tensor_copy(tmv[:, 8 + qt:9 + qt], asmb[:, 25:26])
            VE.tensor_reduce(out=sm2[:, 48 + qt:49 + qt], in_=xt_, axis=AX.X, op=ALU.max)
            VE.tensor_reduce(out=sm2[:, 56 + qt:57 + qt], in_=xt_, axis=AX.X, op=ALU.min)
            vsp(2)
            f_st[qt] = c_dve.bump(VE.last)
            sxA[0] += 1

        def emit_statB():
            if sxB[0] >= sxA[0] or sxB[0] >= 8:
                return
            qt = sxB[0]
            xt_ = xst[:, (qt % 2) * D:(qt % 2 + 1) * D]
            SC.wait_ge(c_dve.h, f_st[qt])
            SC.activation(out=tmv[:, 8 + qt:9 + qt], in_=tmv[:, 8 + qt:9 + qt],
                          func=AF.Ln, bias=EPSC, scale=1.0)
            SC.copy(out=sgs[:, 0:8], in_=small[:, 0:8])
            SC.copy(out=sgs[:, 0:8], in_=small[:, 0:8])
            SC.activation(out=tmv[:, 16 + qt:17 + qt], in_=tmv[:, 8 + qt:9 + qt],
                          func=AF.Exp, bias=ZEROC, scale=-0.5)
            SC.copy(out=sgs[:, 0:8], in_=small[:, 0:8])
            SC.copy(out=sgs[:, 0:8], in_=small[:, 0:8])
            a_r[qt] = c_act.bump(SC.last)
            VE.wait_ge(c_act.h, a_r[qt])
            VE.tensor_tensor(out=sm2[:, 32 + qt:33 + qt], in0=sm2[:, 48 + qt:49 + qt],
                             in1=tmv[:, 0 + qt:1 + qt], op=ALU.subtract)
            VE.tensor_tensor(out=sm2[:, 40 + qt:41 + qt], in0=tmv[:, 0 + qt:1 + qt],
                             in1=sm2[:, 56 + qt:57 + qt], op=ALU.subtract)
            vsp()
            VE.tensor_tensor(out=tmv[:, 24 + qt:25 + qt], in0=sm2[:, 32 + qt:33 + qt],
                             in1=sm2[:, 40 + qt:41 + qt], op=ALU.max)
            vsp()
            VE.tensor_tensor(out=tmv[:, 24 + qt:25 + qt], in0=tmv[:, 24 + qt:25 + qt],
                             in1=tmv[:, 16 + qt:17 + qt], op=ALU.mult)
            vsp()
            VE.tensor_scalar(out=tmv[:, 32 + qt:33 + qt], in0=tmv[:, 24 + qt:25 + qt],
                             scalar1=EPS, scalar2=None, op0=ALU.max)
            vsp()
            VE.reciprocal(sm2[:, 32 + qt:33 + qt], tmv[:, 32 + qt:33 + qt])
            vsp()
            VE.tensor_tensor(out=sm2[:, 40 + qt:41 + qt], in0=sm2[:, 32 + qt:33 + qt],
                             in1=tmv[:, 16 + qt:17 + qt], op=ALU.mult)
            vsp()
            VE.tensor_scalar(out=tmv[:, 40 + qt:41 + qt], in0=sm2[:, 40 + qt:41 + qt],
                             scalar1=127.0, scalar2=None, op0=ALU.mult)
            vsp()
            VE.tensor_scalar(out=sqi[:, :], in0=xt_,
                             scalar1=tmv[:, 0 + qt:1 + qt], scalar2=tmv[:, 40 + qt:41 + qt],
                             op0=ALU.subtract, op1=ALU.mult)
            if qt >= 2:
                VE.wait_ge(c_pe.h, xq_pt[qt - 2])
            VE.tensor_copy(sqb[:, (qt % 2) * D:(qt % 2 + 1) * D], sqi[:, :])
            f_sq[qt] = c_dve.bump(VE.last)
            sxB[0] += 1

        def emit_statC():
            if sxC[0] >= sxB[0] or sxC[0] >= 8:
                return
            qt = sxC[0]
            for g in range(2):
                TE.wait_ge(c_dve.h, f_sq[qt] if g == 0 else p7_cons[0])
                if g == 0:
                    TE.wait_ge(c_dve.h, p7_cons[0] if p7_cons[0] else vtr)
                for j in range(8):
                    k = 8 * g + j
                    TE.transpose(psb7[:, 128 * j:128 * (j + 1)],
                                 sqb[:, (qt % 2) * D + 128 * k:(qt % 2) * D + 128 * (k + 1)],
                                 identb[:, :])
                p_t = c_pe.bump(TE.last)
                xq_pt[qt] = p_t
                VE.wait_ge(c_pe.h, p_t)
                VE.tensor_copy(
                    xq[:, :].rearrange("p (k t) -> p k t", k=NT)[:, 8 * g:8 * g + 8,
                                                                 128 * qt:128 * (qt + 1)],
                    psb7[:, :].rearrange("p (k t) -> p k t", k=8))
                p7_cons[0] = c_dve.bump(VE.last)
            xqd[qt] = p7_cons[0]
            sxC[0] += 1

        def slot_hook(h, p, sub):
            if p == 3 and sub == 2:
                return
            if h == 0:
                if p == 1 and sub == 2:
                    emit_scales()
                elif p >= 2 and f_scales:
                    emit_tern(3 if (p == 3 and sub == 0) else 2 if sub < 2 else 3)
                return
            if h < 3:
                if f_scales:
                    emit_tern(3 if (p == 3 and sub == 0) else 2 if sub < 2 else 3)
                return
            assert t_cur[0] == len(tern_jobs), t_cur[0]
            emit_statC()
            emit_statB()
            emit_statA()
            statA_trig()

        def emit_preamble(hh):
            nonlocal vtr, b6_cons
            for r in range(4):
                if not (hh == 0 and r == 0):
                    TE.wait_ge(c_dve.h, vtr)
                for j in range(8):
                    kt = 8 * r + j
                    TE.transpose(psb7[:, 128 * j:128 * (j + 1)],
                                 qT[:, N * hh + 128 * kt:N * hh + 128 * (kt + 1)], identb[:, :])
                p_tr = c_pe.bump(TE.last)
                VE.wait_ge(c_pe.h, p_tr)
                VE.tensor_copy(vt[:, 1024 * r:1024 * (r + 1)], psb7[:, :])
                vtr = c_dve.bump(VE.last)
            for r8 in range(8):
                VE.tensor_tensor(out=sgs[:, :], in0=vt[:, 512 * r8:512 * (r8 + 1)],
                                 in1=vt[:, 512 * r8:512 * (r8 + 1)], op=ALU.mult)
                VE.tensor_reduce(out=asmb[:, 4 * r8:4 * (r8 + 1)],
                                 in_=sgs[:, :].rearrange("p (k e) -> p k e", k=4),
                                 axis=AX.X, op=ALU.add)
            vsp()
            VE.tensor_reduce(out=asmb[:, 32:33], in_=asmb[:, 0:32], axis=AX.X, op=ALU.max)
            vsp()
            f_nq = c_dve.bump(VE.last)
            TE.wait_ge(c_dve.h, f_nq)
            if b6_cons:
                TE.wait_ge(c_dve.h, b6_cons)
            TE.transpose(bank(6, 128)[0:1, :], asmb[:, 32:33], identf[:, :])
            p_nq = c_pe.bump(TE.last)
            VE.wait_ge(c_pe.h, p_nq)
            VE.tensor_copy(asmb[0:1, 0:128], bank(6, 128)[0:1, :])
            VE.tensor_reduce(out=asmb[0:1, 128:129], in_=asmb[0:1, 0:128], axis=AX.X, op=ALU.max)
            vsp()
            VE.tensor_scalar(out=asmb[0:1, 129:130], in0=asmb[0:1, 128:129],
                             scalar1=-S_ATT, scalar2=None, op0=ALU.mult)
            vsp()
            f_c = c_dve.bump(VE.last)
            b6_cons = f_c
            TE.wait_ge(c_dve.h, f_c)
            TE.matmul(bank(6, 1), onesr[0:1, :], asmb[0:1, 129:130], start=True, stop=True)
            p_c = c_pe.bump(TE.last)
            VE.wait_ge(c_pe.h, p_c)
            VE.tensor_copy(BIASC[hh], bank(6, 1)[:, 0:1])
            vsp()
            f_ch_arr[hh] = c_dve.bump(VE.last)
            b6_cons = f_ch_arr[hh]

        for h in range(NHD):
            sac = sacc[:, 128 * (h % 2):128 * (h % 2) + 128]
            if h == 0:
                TE.wait_ge(c_dve.h, p1_dve)
                emit_preamble(0)
            SC.wait_ge(c_dve.h, f_ch_arr[h])
            if h >= 2:
                SC.wait_ge(c_dve.h, sig_red[h - 2])

            for p in range(4):
                exp_kt = [0] * KT

                def emit_y(kt, exp_kt=exp_kt, p=p):
                    par = kt % 2
                    TE.wait_ge(c_act.h, exp_kt[kt])
                    if kt == 0 and y_evac:
                        TE.wait_ge(c_dve.h, y_evac)
                    if kt == 0 and p == 0 and rb4_gate:
                        TE.wait_ge(c_dve.h, rb4_gate)
                    for j in range(2):
                        TE.matmul(ps[:, 2048 + 512 * j:2048 + 512 * (j + 1)],
                                  vt[:, 128 * kt:128 * (kt + 1)],
                                  pt[:, 1024 * par + 512 * j:1024 * par + 512 * (j + 1)],
                                  start=(kt == 0), stop=(kt == KT - 1))
                    y_last[par] = c_pe.bump(TE.last)

                for kt in range(KT):
                    par = kt % 2
                    sb = 1024 * par
                    for j in range(2):
                        cq = 2 * p + j
                        TE.matmul(ps[:, sb + 512 * j:sb + 512 * (j + 1)],
                                  qT[:, N * h + 128 * kt:N * h + 128 * (kt + 1)],
                                  qT[:, N * h + 512 * cq:N * h + 512 * (cq + 1)],
                                  start=True, stop=True)
                    sc_mk = c_pe.bump(TE.last)
                    SC.wait_ge(c_pe.h, sc_mk)
                    SC.activation(out=pt[:, 1024 * par:1024 * (par + 1)],
                                  in_=ps[:, sb:sb + 1024], func=AF.Exp,
                                  scale=S_ATT, bias=BIASC[h],
                                  accum_out=sac[:, 4 * kt + p:4 * kt + p + 1])
                    exp_kt[kt] = c_act.bump(SC.last)
                    if kt >= 1:
                        emit_y(kt - 1)
                    if kt == 10 or kt == 21:
                        slot_hook(h, p, 0 if kt == 10 else 1)
                emit_y(KT - 1)
                VE.wait_ge(c_pe.h, y_last[(KT - 1) % 2])
                VE.tensor_copy(yraw[:, 1024 * p:1024 * p + 512], bank(4))
                VE.tensor_copy(yraw[:, 1024 * p + 512:1024 * (p + 1)], bank(5))
                y_evac = c_dve.bump(VE.last)
                slot_hook(h, p, 2)
            for _ in range(4):
                SC.copy(out=sgs[:, 0:8], in_=small[:, 0:8])
            a_sig = c_act.bump(SC.last)

            if h > 0:
                dsSW.wait(VE, ms1[h - 1])
            VE.wait_ge(c_act.h, a_sig)
            VE.tensor_reduce(out=scol[:, :], in_=sac.rearrange("p (k e) -> p k e", k=KT),
                             axis=AX.X, op=ALU.add)
            vsp()
            sig_red[h] = c_dve.bump(VE.last)
            VE.reciprocal(rcol[:, :], scol[:, :])
            vsp()
            f_rc = c_dve.bump(VE.last)
            TE.wait_ge(c_dve.h, f_rc)
            if b6_cons:
                TE.wait_ge(c_dve.h, b6_cons)
            TE.transpose(bank(6, 128)[0:32, :], rcol[:, :], identf[:, :])
            p_rt = c_pe.bump(TE.last)
            VE.wait_ge(c_pe.h, p_rt)
            VE.tensor_copy(fsc[0:32, 0:128], bank(6, 128)[0:32, :])
            vsp(2)
            f_rt = c_dve.bump(VE.last)
            b6_cons = f_rt
            SY.wait_ge(c_dve.h, f_rt)
            if h > 0:
                SY.wait_ge(c_pe.h, prb_last[h - 1])
            if h == 0 and dbg.get("fst") is not None:
                SC.wait_ge(c_dve.h, f_rt)
                dsDBG.dma(dbg["fst"][:, :], fsc[0:32, 0:128])
            ms1[h] = dsSW.dma(rowq[0:1, :], fsc[0:32, 0:128])
            if h == 0 and dbg.get("swz") is not None:
                dsSW.wait(SY, ms1[h])
                dsDBG.dma(dbg["swz"][0:1, :], rowq[0:1, :])

            if h + 1 < NHD:
                emit_preamble(h + 1)

            dsSW.wait(TE, ms1[h])
            bb = (bank(6), bank(4))
            rb_cp = [0] * NCH
            for c in range(NCH):
                if c == 0:
                    TE.wait_ge(c_dve.h, b6_cons)
                elif c >= 2:
                    TE.wait_ge(c_dve.h, rb_cp[c - 2])
                TE.matmul(bb[c % 2], onesr[0:1, :], rowq[0:1, 512 * c:512 * (c + 1)],
                          start=True, stop=True)
                p_rb = c_pe.bump(TE.last)
                if c == NCH - 1:
                    prb_last[h] = p_rb
                VE.wait_ge(c_pe.h, p_rb)
                VE.tensor_copy(bcf[:, 512 * c:512 * (c + 1)], bb[c % 2])
                rb_cp[c] = c_dve.bump(VE.last)
            b6_cons = rb_cp[NCH - 2]
            rb4_gate = rb_cp[NCH - 1]
            if h == 0 and dbg.get("bcf") is not None:
                SC.wait_ge(c_dve.h, rb_cp[NCH - 1])
                dsDBG.dma(dbg["bcf"][:, :], bcf[:, :])
                dsDBG.dma(dbg["rowq"][0:1, :], rowq[0:1, :])
                dsDBG.dma(dbg["rcol"][:, :], rcol[:, :])
            if h > 0:
                dsPUB.wait(VE, pub[h - 1])
            VE.tensor_tensor(out=ynb[:, :], in0=yraw[:, :], in1=bcf[:, :], op=ALU.mult)
            f_yn = c_dve.bump(VE.last)
            f_yn_prev = f_yn
            GP.wait_ge(c_dve.h, f_yn)
            pub[h] = dsPUB.dma(
                cc2i[128 * N * h:128 * N * (h + 1)].rearrange("(p t) -> p t", p=128),
                ynb[:, :])
            if dbg.get("yn") is not None:
                dsDBG.dma(dbg["yn"][128 * h:128 * (h + 1), :], ynb[:, :])
                dsDBG.dma(dbg["sig"][128 * h:128 * (h + 1), :], scol[:, :])

        emit_tern(99)
        while sxA[0] < 8:
            statA_trig()
            emit_statA()
        while sxB[0] < 8:
            emit_statB()
        while sxC[0] < 8:
            emit_statC()

        if dbg.get("sml") is not None:
            SC.wait_ge(c_dve.h, c_dve.n)
            dsDBG.dma(dbg["sml"][:, 0:32], small[:, :])
            dsDBG.dma(dbg["sml"][:, 32:192], asmb[:, :])
        dsPUB.wait(GP, pub[NHD - 1])
        cc2d = c_gp.bump(GP.collective_compute(
            "AllGather", ALU.bypass, replica_groups=[[0, 1, 2, 3], [4, 5, 6, 7]],
            ins=[cc2i[:]], outs=[cc2o[:]]))

        if phases < 3:
            GP.wait_ge(c_gp.h, cc2d)
            drain_all()
            return nc, dbg

        f_xq = xqd[7]
        if dbg.get("xq") is not None:
            SC.wait_ge(c_dve.h, f_xq)
            dsDBG.dma(dbg["xq"][:, :], xq[:, :])

        def row_bcast(src_off, dstbuf):
            TE.wait_ge(c_dve.h, c_dve.n)
            TE.transpose(bank(6, 128)[0:8, :], tmv[:, src_off:src_off + 8], identf[:, :])
            p1 = c_pe.bump(TE.last)
            VE.wait_ge(c_pe.h, p1)
            VE.tensor_copy(asmb[0:8, 0:128], bank(6, 128)[0:8, :])
            vsp(2)
            f1 = c_dve.bump(VE.last)
            SY.wait_ge(c_dve.h, f1)
            s2 = dsSW.dma(rowq[0:1, 0:1024], asmb[0:8, 0:128])
            dsSW.wait(TE, s2)
            for cc in range(2):
                TE.matmul(bank(2 + cc), onesr[0:1, :], rowq[0:1, 512 * cc:512 * (cc + 1)],
                          start=True, stop=True)
            p2 = c_pe.bump(TE.last)
            SC.wait_ge(c_pe.h, p2)
            SC.copy(out=dstbuf[:, 0:512], in_=bank(2))
            SC.copy(out=dstbuf[:, 512:1024], in_=bank(3))
            return c_act.bump(SC.last)

        m_up = [0] * 8
        upld = [0] * 8
        upev = {}
        dsSW.wait(VE, ms1[NHD - 1])
        VE.memset(osb[:, :], 0.0)
        VE.memset(fsc[:, :], 0.0)
        c_dve.bump(VE.last)
        SY.wait_ge(c_pe.h, prb_last[NHD - 1])
        for sl in range(8):
            if sl >= 2:
                SY.wait_ge(c_pe.h, upld[sl - 2])
            sb = (sl % 2) * NT * 512
            m_up[sl] = dsUP.dma(upsl[:, sb:sb + NT * 512], tup_h[sl, :, :])
            if sl == 0:
                dsTO.drain(TE)
            dsUP.wait(TE, m_up[sl])
            for ch in range(2):
                for it4 in range(4):
                    it = 4 * sl + it4
                    bkt = 2 * (it4 % 2) + ch
                    key = (it4 % 2, ch)
                    if upev.get(key):
                        TE.wait_ge(c_dve.h, upev[key])
                    if sl == 0 and it4 == 0:
                        TE.wait_ge(c_dve.h, xqd[3] if ch == 0 else xqd[7])
                    for k in range(NT):
                        TE.matmul(bank(bkt),
                                  upsl[:, sb + 512 * k + 128 * it4:sb + 512 * k + 128 * (it4 + 1)],
                                  xq[:, T * k + 512 * ch:T * k + 512 * (ch + 1)],
                                  start=(k == 0), stop=(k == NT - 1))
                    pmk = c_pe.bump(TE.last)
                    if it4 == 3 and ch == 1:
                        upld[sl] = pmk
                    SC.wait_ge(c_pe.h, pmk)
                    SC.activation(out=sgs[:, :], in_=bank(bkt), func=AF.Relu,
                                  bias=ZEROC, scale=1.0)
                    a_rl = c_act.bump(SC.last)
                    VE.wait_ge(c_act.h, a_rl)
                    VE.tensor_tensor(out=v_sb[:, T * it + 512 * ch:T * it + 512 * (ch + 1)],
                                     in0=sgs[:, :], in1=sgs[:, :], op=ALU.mult)
                    vm = osb if ch == 0 else fsc
                    VE.tensor_tensor(out=vm[:, :], in0=vm[:, :],
                                     in1=v_sb[:, T * it + 512 * ch:T * it + 512 * (ch + 1)],
                                     op=ALU.max)
                    upev[key] = c_dve.bump(VE.last)
        f_up = c_dve.n

        TE.wait_ge(c_dve.h, f_up)
        for half, vm in enumerate((osb, fsc)):
            for j in range(4):
                TE.wait_ge(c_dve.h, c_dve.n)
                TE.transpose(bank(6, 128), vm[:, 128 * j:128 * (j + 1)], identf[:, :])
                pj = c_pe.bump(TE.last)
                VE.wait_ge(c_pe.h, pj)
                qq = 4 * half + j
                VE.tensor_reduce(out=tmv[:, 48 + qq:49 + qq], in_=bank(6, 128)[:, 0:128],
                                 axis=AX.X, op=ALU.max)
                c_dve.bump(VE.last)
        vsp()
        c_dve.bump(VE.last)
        VE.tensor_scalar(out=asmb[:, 0:8], in0=tmv[:, 32:40], scalar1=SUP, scalar2=1.0 / 127.0,
                         op0=ALU.mult, op1=ALU.mult)
        vsp()
        VE.tensor_tensor(out=asmb[:, 0:8], in0=asmb[:, 0:8], in1=asmb[:, 0:8], op=ALU.mult)
        vsp()
        VE.tensor_tensor(out=asmb[:, 8:16], in0=asmb[:, 0:8], in1=tmv[:, 48:56], op=ALU.mult)
        vsp()
        VE.tensor_scalar(out=asmb[:, 8:16], in0=asmb[:, 8:16], scalar1=SDN, scalar2=1.0 / 127.0,
                         op0=ALU.mult, op1=ALU.mult)
        vsp()
        VE.tensor_copy(tmv[:, 56:64], asmb[:, 8:16])
        VE.tensor_scalar(out=asmb[:, 16:24], in0=tmv[:, 48:56], scalar1=EPS, scalar2=None,
                         op0=ALU.max)
        vsp()
        VE.reciprocal(asmb[:, 16:24], asmb[:, 16:24])
        vsp()
        VE.tensor_scalar(out=tmv[:, 48:56], in0=asmb[:, 16:24], scalar1=127.0, scalar2=None,
                         op0=ALU.mult)
        vsp()
        f_beta = c_dve.bump(VE.last)
        if dbg.get("tm") is not None:
            SC.wait_ge(c_dve.h, f_beta)
            dsDBG.dma(dbg["tm"][:, :], tmv[:, :])
        a_s2b = row_bcast(48, s2bb)

        hqe = [0] * ITL
        for it in range(ITL):
            if it == 0:
                VE.wait_ge(c_act.h, a_s2b)
            VE.tensor_tensor(out=xsi[:, :], in0=v_sb[:, T * it:T * (it + 1)],
                             in1=s2bb[:, 0:1024], op=ALU.mult)
            VE.tensor_copy(v_sb[:, T * it:T * (it + 1)], xsi[:, :])
            hqe[it] = c_dve.bump(VE.last)
        f_hq = c_dve.n
        if dbg.get("hq") is not None:
            SC.wait_ge(c_dve.h, f_hq)
            dsDBG.dma(dbg["hq"][:, :], v_sb[:, :])

        m_dn = [0] * 4
        dnld = [0] * 4
        dnev = {}
        for dc in range(4):
            gate = dnld[dc - 1] if dc >= 1 else upld[7]
            SY.wait_ge(c_pe.h, gate)
            m_dn[dc] = dsDN.dma(dnsl[:, :], tdn_h[dc, :, :])
            dsDN.wait(TE, m_dn[dc])
            for qt in range(8):
                bkt = 4 + (qt % 2)
                key = qt % 2
                if dnev.get(key):
                    TE.wait_ge(c_dve.h, dnev[key])
                for it in range(ITL):
                    if dc == 0 and qt == 0:
                        TE.wait_ge(c_dve.h, hqe[it])
                    TE.matmul(bank(bkt), v_sb[:, T * it + 128 * qt:T * it + 128 * (qt + 1)],
                              dnsl[:, 512 * it:512 * (it + 1)],
                              start=(it == 0), stop=(it == ITL - 1))
                pmk = c_pe.bump(TE.last)
                if qt == 7:
                    dnld[dc] = pmk
                VE.wait_ge(c_pe.h, pmk)
                VE.tensor_scalar(out=ffo[:, D * qt + 512 * dc:D * qt + 512 * (dc + 1)],
                                 in0=bank(bkt), scalar1=TM_BETA(qt), scalar2=None,
                                 op0=ALU.mult)
                dnev[key] = c_dve.bump(VE.last)
        f_dn = c_dve.n

        SY.wait_ge(c_gp.h, cc2d)
        for arm in nc.switch_core_id(engines=[SY._e], n=512):
            ts_arm = (arm.logical % 8) % 4
            src = cc2o[:].rearrange("(s h p t) -> p (s h) t", s=4, h=NHD, p=128)
            i_y = SY.dma_start(out=yallT[:, :].rearrange("p (g t) -> p g t", g=16),
                               in_=src[:, :, T * ts_arm:T * (ts_arm + 1)])
            i_y.then_inc(dsYA.sems[0], 16)
        dsYA.cnt[0] += 16
        m_yall = (0, dsYA.cnt[0])

        m_bo = dsC.dma(bob[0:1, :], bod[0:1, :])
        dsC.wait(TE, m_bo)
        TE.wait_ge(c_dve.h, f_up)
        bo_cp = [0] * 4
        for j in range(4):
            if j >= 2:
                TE.wait_ge(c_act.h, bo_cp[j - 2])
            TE.matmul(bank(j % 2), onesr[0:1, :], bob[0:1, 512 * j:512 * (j + 1)],
                      start=True, stop=True)
            pj = c_pe.bump(TE.last)
            SC.wait_ge(c_pe.h, pj)
            SC.copy(out=bob[:, 512 * j:512 * (j + 1)], in_=bank(j % 2))
            bo_cp[j] = c_act.bump(SC.last)
        a_bo = bo_cp[3]

        m_wo = [0] * 4
        wold = [0] * 4
        woev = {}
        st_marks = []
        dve_at_tile = []
        m_xt = {}
        GP.wait_ge(c_dve.h, f_dn)
        SC.wait_ge(c_dve.h, f_dn)
        for dc in range(4):
            wb = wob if dc % 2 == 0 else wob2
            if dc >= 2:
                GP.wait_ge(c_pe.h, wold[dc - 2])
                SC.wait_ge(c_pe.h, wold[dc - 2])
            m_wo[dc] = dsWO.dma(
                wb[:, :].rearrange("p (k e) -> p k e", k=NT),
                wod[:, 512 * dc:512 * (dc + 1)].rearrange("(k p) e -> p k e", p=128))
            dsWO.wait(TE, m_wo[dc])
            if dc == 0:
                dsYA.wait(TE, m_yall)
                TE.wait_ge(c_act.h, a_bo)
            for qt in range(8):
                ii = 8 * dc + qt
                slot = ii % 2
                if ii >= 2:
                    SY.wait_ge(c_dve.h, dve_at_tile[ii - 2])
                m_xt[ii] = dsXT.dma(
                    xss[:, 512 * slot:512 * (slot + 1)],
                    x_sl[128 * qt:128 * (qt + 1), 512 * dc:512 * (dc + 1)])
                bkt = 2 + (qt % 2)
                key = qt % 2
                if woev.get(key):
                    TE.wait_ge(c_dve.h, woev[key])
                for k in range(NT):
                    TE.matmul(bank(bkt), yallT[:, T * k + 128 * qt:T * k + 128 * (qt + 1)],
                              wb[:, 512 * k:512 * (k + 1)],
                              start=(k == 0), stop=(k == NT - 1))
                pmk = c_pe.bump(TE.last)
                if qt == 7:
                    wold[dc] = pmk
                ob = osb if ii % 2 == 0 else fsc
                VE.wait_ge(c_pe.h, pmk)
                dsXT.wait(VE, m_xt[ii])
                if len(st_marks) >= 2:
                    dsST.wait(VE, st_marks[-2])
                VE.scalar_tensor_tensor(out=ob[:, :], in0=bank(bkt), scalar=1.0,
                                        in1=bob[:, 512 * dc:512 * (dc + 1)],
                                        op0=ALU.mult, op1=ALU.add)
                VE.tensor_tensor(out=ob[:, :], in0=ob[:, :],
                                 in1=ffo[:, D * qt + 512 * dc:D * qt + 512 * (dc + 1)],
                                 op=ALU.add)
                VE.tensor_tensor(out=ob[:, :], in0=ob[:, :],
                                 in1=xss[:, 512 * slot:512 * (slot + 1)], op=ALU.add)
                woev[key] = c_dve.bump(VE.last)
                dve_at_tile.append(woev[key])
                SC.wait_ge(c_dve.h, woev[key])
                st_marks.append(dsST.dma(
                    outd[128 * qt:128 * (qt + 1), 512 * dc:512 * (dc + 1)], ob[:, :]))

        drain_all()

    return nc, dbg


_CACHE = {}


def _build(phases=3, debug=0):
    key = (phases, debug)
    if key not in _CACHE:
        _CACHE[key] = build_nc(phases=phases, debug=debug)
    return _CACHE[key]


def make_in_maps(x, Wq, Wo, bo, w_up, w_dn):
    wupT = np.ascontiguousarray(w_up.T)
    wdnT = np.ascontiguousarray(w_dn.T)
    idb = np.eye(128, dtype=BF)
    idf = np.eye(128, dtype=np.float32)
    bo2 = np.ascontiguousarray(bo.reshape(1, D))
    maps = []
    for c in range(8):
        b, hg, ts = c // 4, c % 4, c % 4
        xTb = np.ascontiguousarray(x[b].T)
        maps.append({
            "xT": xTb,
            "xslT": np.ascontiguousarray(xTb[:, T * ts:T * (ts + 1)]),
            "x_sl": np.ascontiguousarray(x[b, T * ts:T * (ts + 1), :]),
            "wqd": np.ascontiguousarray(Wq[:, 512 * hg:512 * (hg + 1)]),
            "wod": Wo,
            "wupT": wupT,
            "wdnT": wdnT,
            "bod": bo2,
            "wshu": np.ascontiguousarray(wupT[256 * c:256 * (c + 1), :]),
            "wshd": np.ascontiguousarray(wdnT[512 * c:512 * (c + 1), :]),
            "idbd": idb,
            "idfd": idf,
        })
    return maps


def kernel(L0_attn_x, L0_attn_Wq, L0_attn_Wo, L0_attn_bo, L0_ffn_w_up, L0_ffn_w_down):
    x = np.asarray(L0_attn_x, dtype=np.float32)
    maps = make_in_maps(x, np.asarray(L0_attn_Wq, np.float32),
                        np.asarray(L0_attn_Wo, np.float32),
                        np.asarray(L0_attn_bo, np.float32),
                        np.asarray(L0_ffn_w_up, np.float32),
                        np.asarray(L0_ffn_w_down, np.float32))
    nc, _ = _build(3, 0)
    res = run_bass_kernel_spmd(nc, maps, core_ids=list(range(8)))
    o = np.empty((B, N, D), dtype=np.float32)
    for c in range(8):
        b, ts = c // 4, c % 4
        o[b, T * ts:T * (ts + 1), :] = res.results[c]["outd"]
    return o
